# version 65
# baseline (speedup 1.0000x reference)
import numpy as np
import ml_dtypes
from contextlib import ExitStack

import concourse.bass as bass
import concourse.mybir as mybir
from concourse.bass_utils import run_bass_kernel_spmd

F32 = mybir.dt.float32
BF16 = mybir.dt.bfloat16
I32 = mybir.dt.int32
U8 = mybir.dt.uint8
AF = mybir.ActivationFunctionType
ALU = mybir.AluOpType

P = 128
CH = 8
D = 1024
T = 2048
THALO = 2050
H = 16
NCORE = 8
HID = 3072
HC = 24
SCALE_Q = 0.125
DEQ = 1.0 / 127.0
EPS = 1e-5

G5 = [(i * 410, 410) for i in range(5)]
G4 = [(i * 512, 512) for i in range(4)]

AR_BYTES = 212_832
RA = 0
RB = 65_664
RW = 131_456
RS = 180_608

XB = RA
XP = RA + 32_832
AO = RA

X0H = RB
KB = RB
VB = RB + 32_896
WOSTG = RB
X0R = RB
HBUF = RB
XSQ_F = RB + 49_152
T12_F = RB + 57_344
NSCF = RB + 61_440
SGT = RB + 63_520

STG = RW
WB0 = RW + 16_384
WB1 = RW + 32_768
WVS = STG
SIG = XB
SC123 = XB + 6 * 1_664
QSB = XB + 14_976
TMP = XB + 21_632
WFE = XB + 28_288
AB = RW
XSQ_B = RW + 32_768
T12_B = RW + 40_960
LNC = RW
GBF = RW + 16_384
UBF = RW + 32_768
WDBF = RW
FFO = RW + 24_576
FBT = RW + 40_960

ST_SL = RS
RBMR = ST_SL + 6 * 2_080
RBT = RBMR + 2 * 1_040
WN = RBT + 4 * 2_080
CONST_E16 = WN + 6 * 832
CONST_EB = CONST_E16 + 256
CONST_BIAS = CONST_EB + 2_048
NSCA = CONST_BIAS + 160
ONESC = NSCA + 832
ROWS1 = ONESC + 32
EPSC = ROWS1 + 512
RS_END = EPSC + 32
assert RS_END <= AR_BYTES, RS_END

T12_P = RW + 45_056

KW = 2056


class Sched:
    ENG = ("sync", "tensor", "scalar", "vector", "gpsimd")

    def __init__(self):
        self.q = {e: [] for e in self.ENG}
        self.cnt = {}
        self.last_wait = {}
        self.nemit = 0
        self.trunc = None

    def val(self, sem):
        return self.cnt.get(sem, 0)

    def emit(self, eng, fn, waits=(), inc=None, inc_by=1):
        self.nemit += 1
        if self.trunc is not None and self.nemit > self.trunc:
            return self.cnt.get(inc, 0) + inc_by if inc is not None else None
        newv = None
        if inc is not None:
            self.cnt[inc] = self.cnt.get(inc, 0) + inc_by
            newv = self.cnt[inc]
        w = []
        for sn, sv in waits:
            if not sv:
                continue
            key = (eng, sn)
            if self.last_wait.get(key, -1) >= sv:
                continue
            self.last_wait[key] = sv
            w.append((sn, sv))
        self.q[eng].append((fn, tuple(w), inc, inc_by))
        return newv

    def run(self, eng_name, engine, sems):
        for fn, waits, inc, inc_by in self.q[eng_name]:
            for sn, sv in waits:
                engine.wait_ge(sems[sn], sv)
            ins = fn(engine)
            if inc is not None:
                ins.then_inc(sems[inc], inc_by)


def build_program():
    import os
    _ktrunc = os.environ.get("KTRUNC")
    _kdump = os.environ.get("KDUMP")
    nc = bass.Bass("TRN2", target_bir_lowering=False, debug=False)

    x_in = nc.dram_tensor("x_in", [P, CH, THALO], F32, kind="ExternalInput")
    wts = {w: nc.dram_tensor(w, [P, CH, D], F32, kind="ExternalInput")
           for w in ("wp", "wk", "wv", "wq", "wo")}
    biases = nc.dram_tensor("biases", [P, CH * 5], F32, kind="ExternalInput")
    e16 = nc.dram_tensor("e16", [P, CH * H], BF16, kind="ExternalInput")
    ebc = nc.dram_tensor("ebc", [H, CH * P], BF16, kind="ExternalInput")
    wg = nc.dram_tensor("wg", [HC, P, CH * P], I32, kind="ExternalInput")
    wu = nc.dram_tensor("wu", [HC, P, CH * P], I32, kind="ExternalInput")
    wdt = nc.dram_tensor("wdt", [CH, P, HC * P], I32, kind="ExternalInput")
    out = nc.dram_tensor("out", [P, CH, T], F32, kind="ExternalOutput")
    gscr = nc.dram_tensor("gscr", [P, HC * 1024], BF16, kind="Internal")
    uscr = nc.dram_tensor("uscr", [P, HC * 1024], BF16, kind="Internal")
    dscr = nc.dram_tensor("dscr", [P, CH * 3072], BF16, kind="Internal")

    s = Sched()
    if _ktrunc:
        s.trunc = int(_ktrunc)

    with ExitStack() as st:
        en = st.enter_context
        arena = en(nc.sbuf_tensor("arena", [P, AR_BYTES], U8))
        a16 = arena.bitcast(BF16)
        a32 = arena.bitcast(F32)
        ai32 = arena.bitcast(I32)
        pb = [en(nc.psum_tensor(f"pb{i}", [P, 512], F32)) for i in range(8)]
        sems = {n: en(nc.semaphore(n)) for n in ("d", "t", "a", "v", "g", "o", "w", "q")}
        block = en(nc.Block())

        def v16(off, p0, np_, n):
            assert off % 2 == 0 and off + 2 * n <= AR_BYTES
            return a16[p0:p0 + np_, off // 2: off // 2 + n]

        def v32(off, p0, np_, n):
            assert off % 4 == 0 and off + 4 * n <= AR_BYTES
            return a32[p0:p0 + np_, off // 4: off // 4 + n]

        def vi32(off, p0, np_, n):
            assert off % 4 == 0 and off + 4 * n <= AR_BYTES
            return ai32[p0:p0 + np_, off // 4: off // 4 + n]

        ndma = [0]

        def dma(dst_fn, src_fn, waits=()):
            ndma[0] += 1

            def fn(e):
                return e.dma_start(out=dst_fn(), in_=src_fn())
            s.emit("sync", fn, waits=waits, inc="d", inc_by=16)
            return 16 * ndma[0]

        nw = [0]

        def wdma(dst_fn, src_fn, waits=()):
            nw[0] += 1

            def fn(e):
                return e.dma_start(out=dst_fn(), in_=src_fn())
            s.emit("gpsimd", fn, waits=waits, inc="w", inc_by=16)
            return 16 * nw[0]

        nq = [0]

        def sdma(dst_fn, src_fn, waits=()):
            nq[0] += 1

            def fn(e):
                return e.dma_start(out=dst_fn(), in_=src_fn())
            s.emit("sync", fn, waits=waits, inc="q", inc_by=16)
            return 16 * nq[0]

        w_xb1 = wdma(lambda: v16(XB, 0, P, CH * 2052).rearrange("p (c m) -> p c m", c=CH)[:, :, 0:412],
                     lambda: x_in[:, :, 0:412])
        w_xb = wdma(lambda: v16(XB, 0, P, CH * 2052).rearrange("p (c m) -> p c m", c=CH)[:, :, 412:THALO],
                    lambda: x_in[:, :, 412:THALO])

        d_e16 = dma(lambda: v16(CONST_E16, 0, P, CH * H), lambda: e16[:])
        d_eb = dma(lambda: v16(CONST_EB, 0, H, CH * P), lambda: ebc[:])
        d_bias = dma(lambda: v32(CONST_BIAS, 0, P, CH * 5), lambda: biases[:])

        def fn_mc(e):
            e.memset(v16(ONESC, 0, P, 1), 1.0)
            e.memset(v16(ROWS1, 0, 1, P), 1.0)
            e.memset(v16(ROWS1, 32, 1, P), 1.0)
            e.memset(v32(EPSC, 0, 1, 1), EPS)
            e.memset(v16(ROWS1 + 256, 32, 1, P), -1.0)
            return e.memset(v16(ROWS1 + 256, 0, 1, P), -1.0)
        v_consts = s.emit("vector", fn_mc, inc="v")

        def fn_warm(e):
            return e.activation(v16(NSCA, 0, 1, 1), v16(ONESC, 0, 1, 1), AF.Square)
        s.emit("scalar", fn_warm, waits=[("v", v_consts)], inc="a")

        def load_attn_weight(name, wbuf_off, pe_free=0):
            w = [("t", pe_free)] if pe_free else []
            return wdma(lambda: v16(wbuf_off, 0, P, 8192).rearrange("p (c m) -> p c m", c=CH),
                        lambda: wts[name][:], waits=w)

        w_wp = load_attn_weight("wp", WB0)
        w_wk = load_attn_weight("wk", WB1)

        gd = {"rbt_v0": 0, "rbt_v1": 0, "rbt_g0": 0, "rbt_g1": 0, "xsq_t": 0,
              "stats_v0": 0, "stats_v1": 0, "rb_t0": 0, "rb_t1": 0, "mr_t0": 0, "mr_t1": 0}
        for _b in range(8):
            gd[f"bankev_{_b}"] = 0
            gd[f"bankev_v{_b}"] = 0

        def emit_ln(src_off, src_stride_el, grid, wait_src_fn, xsq_off, sums_banks, tile_cb,
                    n_chunks=8, skip_square=False, sq_dve=False):
            tiles = []
            for tix, (t0, W) in enumerate(grid):
                par = tix % 2
                pbase = 32 * par
                if len(sums_banks) == 4:
                    bs, bq = sums_banks[2 * par], sums_banks[2 * par + 1]
                else:
                    bs, bq = sums_banks
                wait_src = wait_src_fn(tix)
                sq_last = None
                sq_sem = "v" if sq_dve else "a"
                if not skip_square:
                    for c in range(n_chunks):
                        if sq_dve:
                            def fn(e, c=c, t0=t0, W=W):
                                src_ap = v16(src_off + c * 2 * src_stride_el + 2 * t0, 0, P, W)
                                return e.tensor_tensor(v16(xsq_off + c * 2 * 512, 0, P, W),
                                                       src_ap, src_ap, ALU.mult)
                            sq_last = s.emit("vector", fn,
                                             waits=[wait_src, ("t", gd["xsq_t"])], inc="v")
                        else:
                            def fn(e, c=c, t0=t0, W=W):
                                return e.activation(v16(xsq_off + c * 2 * 512, 0, P, W),
                                                    v16(src_off + c * 2 * src_stride_el + 2 * t0,
                                                        0, P, W), AF.Square)
                            sq_last = s.emit("scalar", fn,
                                             waits=[wait_src, ("t", gd["xsq_t"])], inc="a")

                def fn_sum(e, t0=t0, W=W, bs=bs):
                    ins = None
                    for c in range(n_chunks):
                        ins = e.matmul(pb[bs][0:1, 0:W], v16(ONESC, 0, P, 1),
                                       v16(src_off + c * 2 * src_stride_el + 2 * t0, 0, P, W),
                                       start=(c == 0), stop=(c == n_chunks - 1))
                    return ins
                t_sum = s.emit("tensor", fn_sum,
                               waits=[wait_src, ("a", gd[f"bankev_{bs}"]),
                                      ("v", gd[f"bankev_v{bs}"]), ("v", v_consts)],
                               inc="t")

                def fn_ssq(e, W=W, bq=bq):
                    ins = None
                    for c in range(n_chunks):
                        ins = e.matmul(pb[bq][0:1, 0:W], v16(ONESC, 0, P, 1),
                                       v16(xsq_off + c * 2 * 512, 0, P, W),
                                       start=(c == 0), stop=(c == n_chunks - 1))
                    return ins
                t_ssq = s.emit("tensor", fn_ssq,
                               waits=[(sq_sem, sq_last if sq_last else 0), wait_src,
                                      ("a", gd[f"bankev_{bq}"]),
                                      ("v", gd[f"bankev_v{bq}"])], inc="t")
                gd["xsq_t"] = t_ssq

                m_of, nm_of, vv_of = (ST_SL + 2 * 2_080, ST_SL + 3 * 2_080, ST_SL + 4 * 2_080)
                inv_d = 1.0 / D

                def fn_m(e, W=W, bs=bs, pbase=pbase):
                    return e.tensor_scalar_mul(v32(m_of, pbase, 1, W), pb[bs][0:1, 0:W], inv_d)
                v_m = s.emit("vector", fn_m,
                             waits=[("t", t_sum), ("v", gd[f"stats_v{par}"])], inc="v")
                gd[f"bankev_v{bs}"] = v_m

                def fn_nm(e, W=W, pbase=pbase):
                    return e.scalar_tensor_tensor(v32(nm_of, pbase, 1, W), v32(m_of, pbase, 1, W),
                                                  -1.0, v32(m_of, pbase, 1, W),
                                                  ALU.mult, ALU.mult)
                s.emit("vector", fn_nm, inc="v")

                def fn_vv(e, W=W, bq=bq, pbase=pbase):
                    return e.scalar_tensor_tensor(v32(vv_of, pbase, 1, W),
                                                  pb[bq][0:1, 0:W], inv_d,
                                                  v32(nm_of, pbase, 1, W), ALU.mult, ALU.add)
                v_vv = s.emit("vector", fn_vv, waits=[("t", t_ssq)], inc="v")
                gd[f"bankev_v{bq}"] = v_vv

                def fn_ln_(e, W=W, pbase=pbase):
                    return e.activation(v32(vv_of, pbase, 1, W), v32(vv_of, pbase, 1, W), AF.Ln,
                                        bias=v32(EPSC, 0, 1, 1))
                s.emit("scalar", fn_ln_, waits=[("v", v_vv), ("v", v_consts)], inc="a")

                def fn_exp(e, W=W, pbase=pbase):
                    return e.activation(v16(RBMR, pbase, 1, W), v32(vv_of, pbase, 1, W),
                                        AF.Exp, scale=-0.5)
                a_rb = s.emit("scalar", fn_exp, waits=[("t", gd[f"rb_t{par}"])], inc="a")

                def fn_mr(e, W=W, pbase=pbase):
                    return e.tensor_tensor(v16(RBMR + 1_040, pbase, 1, W), v32(m_of, pbase, 1, W),
                                           v16(RBMR, pbase, 1, W), ALU.mult)
                v_mr = s.emit("vector", fn_mr, waits=[("a", a_rb), ("t", gd[f"mr_t{par}"])], inc="v")
                gd[f"stats_v{par}"] = v_mr

                def fn_bR(e, W=W, bs=bs, pbase=pbase):
                    return e.matmul(pb[bs][:, 0:W], v16(ROWS1, pbase, 1, P),
                                    v16(RBMR, pbase, 1, W), start=True, stop=True)
                t_bR = s.emit("tensor", fn_bR, waits=[("a", a_rb), ("v", v_m)], inc="t")
                gd[f"rb_t{par}"] = t_bR

                def fn_bB(e, W=W, bq=bq, pbase=pbase):
                    return e.matmul(pb[bq][:, 0:W], v16(ROWS1 + 256, pbase, 1, P),
                                    v16(RBMR + 1_040, pbase, 1, W), start=True, stop=True)
                t_bB = s.emit("tensor", fn_bB, waits=[("v", v_mr)], inc="t")
                gd[f"mr_t{par}"] = t_bB
                td = dict(t0=t0, W=W, t_bR=t_bR, t_bB=t_bB, par=par, bs=bs, bq=bq)
                tile_cb(td)
                tiles.append(td)
            return tiles

        def emit_rb_evac(td, banks, as_f32):
            del banks
            W = td["W"]
            par = td.get("par", 0)
            vv = v32 if as_f32 else v16
            rslot = RBT + par * 4_160
            bslot = RBT + par * 4_160 + 2_080

            def fn_evR(e, td=td, W=W, rslot=rslot):
                return e.activation(vv(rslot, 0, P, W), pb[td["bs"]][:, 0:W], AF.Copy)
            a_evR = s.emit("scalar", fn_evR,
                           waits=[("t", td["t_bR"]), ("v", gd[f"rbt_v{par}"]),
                                  ("g", gd[f"rbt_g{par}"])], inc="a")
            gd[f"bankev_{td['bs']}"] = a_evR

            def fn_evB(e, td=td, W=W, bslot=bslot):
                return e.activation(vv(bslot, 0, P, W), pb[td["bq"]][:, 0:W], AF.Copy)
            a_evB = s.emit("scalar", fn_evB, waits=[("t", td["t_bB"])], inc="a")
            gd[f"bankev_{td['bq']}"] = a_evB
            td["rslot"] = rslot
            td["bslot"] = bslot
            return a_evR, a_evB

        lna_state = {"v": 0}
        lna_pend = [None]

        def lna_cb(td):
            if lna_pend[0] is not None:
                fn, lna_pend[0] = lna_pend[0], None
                fn()
            t0, W = td["t0"], td["W"]
            a_evR, a_evB = emit_rb_evac(td, None, as_f32=False)
            rs_, bs_ = td["rslot"], td["bslot"]
            lna_pend[0] = lambda: lna_norm(td, t0, W, rs_, bs_, a_evR, a_evB)

        def lna_norm(td, t0, W, rs_, bs_, a_evR, a_evB):
            v_last = 0
            for c in range(CH):
                def fn_n1(e, c=c, t0=t0, W=W, rs_=rs_):
                    return e.tensor_tensor(v16(NSCA, 0, P, W),
                                           v16(XB + c * 2 * 2052 + 2 * t0, 0, P, W),
                                           v16(rs_, 0, P, W), ALU.mult)
                s.emit("vector", fn_n1, waits=[("a", a_evR), ("a", a_evB)], inc="v")

                def fn_n2(e, c=c, t0=t0, W=W, bs_=bs_):
                    return e.tensor_tensor(v16(XB + c * 2 * 2052 + 2 * t0, 0, P, W),
                                           v16(NSCA, 0, P, W),
                                           v16(bs_, 0, P, W), ALU.add)
                v_last = s.emit("vector", fn_n2, inc="v")
            td["v_norm"] = v_last
            gd[f"rbt_v{td['par']}"] = v_last
            lna_state["v"] = v_last

        lna_tiles = emit_ln(XB, 2052, G5,
                            lambda tix: ("w", w_xb1 if tix == 0 else w_xb),
                            XP, (6, 7, 4, 5), lna_cb)
        fn, lna_pend[0] = lna_pend[0], None
        fn()
        v_lna = lna_state["v"]

        def emit_proj_tile(state, wbuf_off, rhs_fn, dst_fn, ti, t0, W, tile_waits,
                           mm_extra=(), ev_extra=(), banks=(0, 1)):
            a_evs = state["a_evs"]
            for oc in range(CH):
                bank = banks[len(a_evs) % 2]
                waits = list(tile_waits) + list(mm_extra)
                if len(a_evs) >= 2:
                    waits.append(("a", a_evs[-2]))
                else:
                    waits.append(("a", s.val("a")))

                def fn_mm(e, oc=oc, t0=t0, W=W, bank=bank):
                    ins = None
                    for ic in range(CH):
                        ins = e.matmul(pb[bank][0:P, 0:W],
                                       v16(wbuf_off + 2 * (ic * 1024 + oc * 128), 0, P, 128),
                                       rhs_fn(ic, t0, W),
                                       start=(ic == 0), stop=(ic == CH - 1))
                    return ins
                state["t_last"] = s.emit("tensor", fn_mm, waits=waits, inc="t")

                def fn_ev(e, oc=oc, t0=t0, W=W, bank=bank):
                    return dst_fn(e, oc, t0, W, pb[bank])
                a_evs.append(s.emit("scalar", fn_ev,
                                    waits=[("t", state["t_last"])] + list(ev_extra), inc="a"))

        def emit_proj(wbuf_off, bias_idx, rhs_fn, dst_fn, grid, tile_wait_fn,
                      mm_extra=(), ev_extra=(), banks=(0, 1)):
            state = {"a_evs": [], "t_last": 0}
            for ti, (t0, W) in enumerate(grid):
                emit_proj_tile(state, wbuf_off, rhs_fn, dst_fn, ti, t0, W, tile_wait_fn(ti),
                               mm_extra=mm_extra, ev_extra=ev_extra, banks=banks)
            return state["t_last"], state["a_evs"]

        def bias_ap(w_idx, oc):
            return v32(CONST_BIAS + 4 * (oc * 5 + w_idx), 0, P, 1)

        def rhs_lna(ic, t0, W):
            return v16(XB + ic * 2 * 2052 + 2 * t0, 0, P, W)

        def rhs_xp(ic, t0, W):
            return v16(XP + ic * 2 * 2052 + 2 * t0, 0, P, W)

        def dst_xp(e, oc, t0, W, ps):
            return e.activation(v16(XP + oc * 2 * 2052 + 2 * t0, 0, P, W), ps[0:P, 0:W],
                                AF.Identity, bias=bias_ap(0, oc))
        t_xp, a_xp_evs = emit_proj(WB0, 0, rhs_lna, dst_xp, G5,
                                   lambda ti: [("v", lna_tiles[ti]["v_norm"])],
                                   mm_extra=[("w", w_wp), ("d", d_bias)])

        w_wv = load_attn_weight("wv", WVS)

        def fn_guards(e):
            e.memset(a16[:, KB // 2: KB // 2 + CH * KW: KW], 0.0)
            e.memset(a16[:, KB // 2 + 2051: KB // 2 + CH * KW: KW], 0.0)
            e.memset(a16[:, VB // 2: VB // 2 + CH * KW: KW], 0.0)
            return e.memset(a16[:, VB // 2 + 2051: VB // 2 + CH * KW: KW], 0.0)
        v_guards = s.emit("vector", fn_guards, inc="v")

        def dst_k(e, oc, t0, W, ps):
            return e.activation(v16(KB + oc * 2 * KW + 2 * (t0 + 1), 0, P, W), ps[0:P, 0:W],
                                AF.Identity, bias=bias_ap(2, oc))

        def dst_v(e, oc, t0, W, ps):
            return e.activation(v16(VB + oc * 2 * KW + 2 * (t0 + 1), 0, P, W), ps[0:P, 0:W],
                                AF.Identity, bias=bias_ap(3, oc))

        w_wq = load_attn_weight("wq", WB0, pe_free=t_xp)

        ps_state = {"a_evs": [], "t_last": 0}

        v_subs_prev = 0
        v_win_done = 0
        win_done_at = []
        wfe_trio_guard = {5: 0, 2: 0}
        wn_guard = [0, 0]
        t_red_last = 0
        a_q_evs = ps_state["a_evs"]
        win_pp = 0
        pending_window = []
        t_k_last = 0
        t_v_last = 0
        qk_last = [0]

        def super_tiles():
            nonlocal t_k_last, t_v_last
            for ti, (t0, W) in enumerate(G5):
                emit_proj_tile(ps_state, WB1, rhs_xp, dst_k, ti, t0, W,
                               [("a", a_xp_evs[ti * CH + CH - 1])],
                               mm_extra=[("w", w_wk)], ev_extra=[("v", v_guards)])
                t_k_last = ps_state["t_last"]
                emit_proj_tile(ps_state, WVS, rhs_xp, dst_v, ti, t0, W,
                               [("a", a_xp_evs[ti * CH + CH - 1])],
                               mm_extra=[("w", w_wv)], ev_extra=[("v", v_guards)])
                t_v_last = ps_state["t_last"]
                if ti >= 1:
                    yield ti - 1
            yield len(G5) - 1

        for ti in super_tiles():
            t0, W = G5[ti]
            q_evs = []
            for c in range(CH):
                bank = len(a_q_evs) % 2
                waits = [("w", w_wq), ("a", a_xp_evs[ti * CH + CH - 1])]
                if len(a_q_evs) >= 2:
                    waits.append(("a", a_q_evs[-2]))
                else:
                    waits.append(("a", s.val("a")))

                def fn_qmm(e, c=c, t0=t0, W=W, bank=bank):
                    ins = None
                    for ic in range(CH):
                        ins = e.matmul(pb[bank][0:P, 0:W],
                                       v16(WB0 + 2 * (ic * 1024 + c * 128), 0, P, 128),
                                       rhs_xp(ic, t0, W),
                                       start=(ic == 0), stop=(ic == CH - 1))
                    return ins
                t_q = s.emit("tensor", fn_qmm, waits=waits, inc="t")

                def fn_qev(e, c=c, W=W, bank=bank):
                    return e.activation(v16(QSB + c * 832, 0, P, W), pb[bank][0:P, 0:W],
                                        AF.Identity, bias=bias_ap(1, c))
                a_q = s.emit("scalar", fn_qev, waits=[("t", t_q)], inc="a")
                q_evs.append(a_q)
                a_q_evs.append(a_q)

            for n in range(3):
                v_tm = 0
                for c in range(CH):
                    def fn_tm(e, c=c, n=n, t0=t0, W=W):
                        return e.tensor_tensor(v16(TMP + c * 832, 0, P, W),
                                               v16(QSB + c * 832, 0, P, W),
                                               v16(KB + c * 2 * KW + 2 * (t0 + n), 0, P, W),
                                               ALU.mult)
                    v_tm = s.emit("vector", fn_tm,
                                  waits=[("a", q_evs[c]), ("t", t_red_last)], inc="v")
                qk_last[0] = v_tm

                def fn_red(e, n=n, W=W):
                    ins = None
                    for c in range(CH):
                        ins = e.matmul(pb[2 + n][0:H, 0:W],
                                       v16(CONST_E16 + c * 2 * H, 0, P, H),
                                       v16(TMP + c * 832, 0, P, W),
                                       start=(c == 0), stop=(c == CH - 1))
                    return ins
                t_red_last = s.emit("tensor", fn_red,
                                    waits=[("v", v_tm), ("v", v_subs_prev), ("d", d_e16),
                                           ("v", wfe_trio_guard[2])], inc="t")

            SG = [SIG + i * 1_664 for i in range(6)]

            def fn_s0(e, W=W):
                return e.activation(v32(SG[0], 0, H, W), pb[2][0:H, 0:W], AF.Copy)
            a_s0 = s.emit("scalar", fn_s0, waits=[("t", t_red_last)], inc="a")

            def fn_mx1(e, W=W):
                return e.tensor_tensor(v32(SG[1], 0, H, W), v32(SG[0], 0, H, W),
                                       pb[3][0:H, 0:W], ALU.max)
            s.emit("vector", fn_mx1, waits=[("a", a_s0), ("t", t_red_last)], inc="v")

            def fn_mx2(e, W=W):
                return e.tensor_tensor(v32(SG[2], 0, H, W), v32(SG[1], 0, H, W),
                                       pb[4][0:H, 0:W], ALU.max)
            s.emit("vector", fn_mx2, inc="v")
            for n in range(3):
                def fn_sub(e, n=n, W=W):
                    return e.tensor_tensor(v32(SG[3 + n], 0, H, W), pb[2 + n][0:H, 0:W],
                                           v32(SG[2], 0, H, W), ALU.subtract)
                v_subs_prev = s.emit("vector", fn_sub, inc="v")
            if pending_window:
                v_win_done = pending_window.pop(0)(v_win_done, v_subs_prev)

            def fn_exp(e, W=W):
                return e.activation(v32(SG[3], 0, H, 3 * 416), v32(SG[3], 0, H, 3 * 416),
                                    AF.Exp, scale=SCALE_Q)
            a_exp = s.emit("scalar", fn_exp, waits=[("v", v_subs_prev)], inc="a")

            def fn_ad1(e, W=W):
                return e.tensor_tensor(v32(SG[0], 0, H, W), v32(SG[3], 0, H, W),
                                       v32(SG[4], 0, H, W), ALU.add)
            s.emit("vector", fn_ad1, waits=[("a", a_exp)], inc="v")

            def fn_ad2(e, W=W):
                return e.tensor_tensor(v32(SG[1], 0, H, W), v32(SG[0], 0, H, W),
                                       v32(SG[5], 0, H, W), ALU.add)
            s.emit("vector", fn_ad2, inc="v")

            def fn_rec(e, W=W):
                return e.reciprocal(v32(SG[0], 0, H, W), v32(SG[1], 0, H, W))
            s.emit("vector", fn_rec, inc="v")

            wslots = [WN + (win_pp * 3 + n) * 832 for n in range(3)]
            v_wn = 0
            for n in range(3):
                def fn_wn(e, n=n, W=W, wslots=wslots):
                    return e.tensor_tensor(v16(wslots[n], 0, H, W), v32(SG[3 + n], 0, H, W),
                                           v32(SG[0], 0, H, W), ALU.mult)
                v_wn = s.emit("vector", fn_wn, waits=[("t", wn_guard[win_pp])], inc="v")

            def emit_window(vw_in, subs_now, t0=t0, W=W, wslots=wslots, v_wn=v_wn,
                            win_pp=win_pp):
                vw = vw_in
                jlo = t0
                jhi = t0 + W
                WW = jhi - jlo
                t_wf = 0
                trio_v = dict(wfe_trio_guard)
                for c in range(CH):
                    b0 = 5 if c % 2 == 0 else 2
                    for n in range(3):
                        def fn_wf(e, c=c, n=n, b0=b0, jlo=jlo, WW=WW, wslots=wslots, t0=t0):
                            return e.matmul(pb[b0 + n][0:P, 0:WW],
                                            v16(CONST_EB + c * 2 * P, 0, H, P),
                                            v16(wslots[n] + 2 * (jlo - t0), 0, H, WW),
                                            start=True, stop=True)
                        wfw = [("v", v_wn), ("v", trio_v[b0]), ("d", d_eb)]
                        if b0 == 2:
                            wfw.append(("v", subs_now))
                        t_wf = s.emit("tensor", fn_wf, waits=wfw, inc="t")
                    SCa, SCb, SCc = SC123, SC123 + 1_664, SC123 + 2 * 1_664

                    def fn_w1(e, c=c, b0=b0, jlo=jlo, WW=WW):
                        return e.tensor_tensor(v32(SCa, 0, P, WW), pb[b0][0:P, 0:WW],
                                               v16(VB + c * 2 * KW + 2 * jlo, 0, P, WW), ALU.mult)
                    s.emit("vector", fn_w1,
                           waits=[("t", t_wf), ("a", a_q_evs[-1]), ("v", vw)], inc="v")

                    def fn_w2(e, c=c, b0=b0, jlo=jlo, WW=WW):
                        return e.tensor_tensor(v32(SCb, 0, P, WW), pb[b0 + 1][0:P, 0:WW],
                                               v16(VB + c * 2 * KW + 2 * (jlo + 1), 0, P, WW), ALU.mult)
                    s.emit("vector", fn_w2, inc="v")

                    def fn_w3(e, WW=WW):
                        return e.tensor_tensor(v32(SCc, 0, P, WW), v32(SCa, 0, P, WW),
                                               v32(SCb, 0, P, WW), ALU.add)
                    s.emit("vector", fn_w3, inc="v")

                    def fn_w4(e, c=c, b0=b0, jlo=jlo, WW=WW):
                        return e.tensor_tensor(v32(SCa, 0, P, WW), pb[b0 + 2][0:P, 0:WW],
                                               v16(VB + c * 2 * KW + 2 * (jlo + 2), 0, P, WW), ALU.mult)
                    v_w4 = s.emit("vector", fn_w4, inc="v")
                    trio_v[b0] = v_w4
                    wfe_trio_guard[b0] = v_w4

                    def fn_w5(e, c=c, jlo=jlo, WW=WW):
                        return e.tensor_tensor(v16(VB + c * 2 * KW + 2 * (jlo - 1), 0, P, WW),
                                               v32(SCc, 0, P, WW),
                                               v32(SCa, 0, P, WW), ALU.add)
                    vw = s.emit("vector", fn_w5, inc="v")

                wn_guard[win_pp] = t_wf
                win_done_at.append(vw)
                return vw

            pending_window.append(emit_window)
            win_pp ^= 1

        while pending_window:
            v_win_done = pending_window.pop(0)(v_win_done, v_subs_prev)

        wb_gu = {}
        wb_wd = {}
        for sb in range(6):
            for kind, srcw, scr in (("g", wg, gscr), ("u", wu, uscr)):
                slot = WVS + (0 if kind == "g" else 8_192)
                w = [("t", t_v_last)]
                if sb >= 1:
                    w.append(("q", wb_gu[(sb - 1, kind)]))
                cw = wdma(
                    lambda slot=slot: v16(slot, 0, P, 4096).rearrange(
                        "p (h m) -> p h m", h=4),
                    lambda srcw=srcw, sb=sb: srcw[4 * sb:4 * sb + 4, :, :].rearrange(
                        "h p m -> p h m"),
                    waits=w)
                wb_gu[(sb, kind)] = sdma(
                    lambda scr=scr, sb=sb: scr[:, 4096 * sb:4096 * sb + 4096],
                    lambda slot=slot: v16(slot, 0, P, 4096),
                    waits=[("w", cw)])
        w_wo = load_attn_weight("wo", WB1, pe_free=max(t_red_last, t_k_last))
        for pi, oc0 in enumerate((2, 4, 6, 0)):
            slot = KB + (pi % 2) * 12_288
            w = [("v", qk_last[0])]
            if pi >= 2:
                w.append(("q", wb_wd[pi - 2]))
            cw = wdma(
                lambda slot=slot: v16(slot, 0, P, 6144).rearrange(
                    "p (o m) -> p o m", o=2),
                lambda oc0=oc0: wdt[oc0:oc0 + 2, :, :].rearrange("o p m -> p o m"),
                waits=w)
            wb_wd[pi] = sdma(
                lambda oc0=oc0: dscr[:, 3072 * oc0:3072 * oc0 + 6144],
                lambda slot=slot: v16(slot, 0, P, 6144),
                waits=[("w", cw)])

        def rhs_vout(ic, t0, W):
            return v16(VB + ic * 2 * KW + 2 * t0, 0, P, W)

        def dst_ao(e, oc, t0, W, ps):
            return e.activation(v32(AO + oc * 4 * 2048 + 4 * t0, 0, P, W), ps[0:P, 0:W],
                                AF.Identity, bias=bias_ap(4, oc))
        t_wo, a_wo_evs = emit_proj(WB1, 4, rhs_vout, dst_ao, G4,
                                   lambda ti: [("v", win_done_at[4])],
                                   mm_extra=[("w", w_wo)],
                                   ev_extra=[("q", wb_gu[(5, "u")])])
        gd["bankev_0"] = a_wo_evs[-2]
        gd["bankev_1"] = a_wo_evs[-1]
        ab_v = []
        for ti2, (t0w, Ww) in enumerate(G4):
            vlast_ab = 0
            for oc in range(CH):
                def fn_ab(e, oc=oc, t0w=t0w, Ww=Ww):
                    return e.tensor_copy(v16(AB + oc * 2 * 2048 + 2 * t0w, 0, P, Ww),
                                         v32(AO + oc * 4 * 2048 + 4 * t0w, 0, P, Ww))
                vlast_ab = s.emit("vector", fn_ab,
                                  waits=[("a", a_wo_evs[ti2 * CH + oc])], inc="v")
            ab_v.append(vlast_ab)

        w_x0a = wdma(lambda: v32(X0R + 3 * 4 * 2048, 0, P, 5 * 2048).rearrange(
                         "p (c m) -> p c m", c=5),
                     lambda: x_in[:, 3:8, 1:2049], waits=[("t", t_wo)])
        d_x0r_b = dma(lambda: v32(X0R, 0, P, 3 * 2048).rearrange("p (c m) -> p c m", c=3),
                      lambda: x_in[:, 0:3, 1:2049],
                      waits=[("t", t_wo), ("q", wb_wd[3])])

        lnb_state = {"v": 0, "g": 0, "v67": 0, "t3": None, "tiles": []}
        NPOOL = 0

        lnb_pend = [None]

        def lnb_cb(td):
            t0, W = td["t0"], td["W"]
            if t0 != 1536 and lnb_pend[0] is not None:
                fn, lnb_pend[0] = lnb_pend[0], None
                fn()
            a_evR, a_evB = emit_rb_evac(td, None, as_f32=True)
            rs_, bs_ = td["rslot"], td["bslot"]
            if t0 == 1536:
                lnb_state["t3"] = (td, t0, W, rs_, bs_, a_evR, a_evB)
            else:
                lnb_pend[0] = lambda: lnb_resid(td, t0, W, rs_, bs_, a_evR, a_evB)

        def lnb_resid(td, t0, W, rs_, bs_, a_evR, a_evB,
                      chunks=(6, 7, 0, 1, 2, 3, 4, 5), final=True):
            v_last = 0
            g_last = 0
            for c in chunks:
                pool = c >= CH - NPOOL
                eng, sem, sc = ("gpsimd", "g", T12_P) if pool else ("vector", "v", T12_B)

                def fn_r1(e, c=c, t0=t0, W=W, rs_=rs_, sc=sc):
                    return e.tensor_tensor(v32(sc, 0, P, W),
                                           v32(AO + c * 4 * 2048 + 4 * t0, 0, P, W),
                                           v32(rs_, 0, P, W), ALU.mult)
                wx = ("d", d_x0r_b) if c < 3 else ("w", w_x0a)
                s.emit(eng, fn_r1,
                       waits=[("a", a_evR), ("a", a_evB), wx], inc=sem)

                def fn_r2(e, W=W, bs_=bs_, sc=sc):
                    return e.tensor_tensor(v32(sc + 2_048, 0, P, W), v32(sc, 0, P, W),
                                           v32(bs_, 0, P, W), ALU.add)
                s.emit(eng, fn_r2, inc=sem)

                def fn_r3(e, c=c, t0=t0, W=W, sc=sc):
                    return e.tensor_tensor(v32(AO + c * 4 * 2048 + 4 * t0, 0, P, W),
                                           v32(sc + 2_048, 0, P, W),
                                           v32(X0R + c * 4 * 2048 + 4 * t0, 0, P, W), ALU.add)
                last = s.emit(eng, fn_r3, inc=sem)
                if pool:
                    g_last = last
                else:
                    v_last = last
                if c == 7 and not pool:
                    lnb_state["v67"] = last
            if not final:
                return
            gd[f"rbt_v{td['par']}"] = v_last
            gd[f"rbt_g{td['par']}"] = g_last
            lnb_state["v"] = v_last
            lnb_state["g"] = g_last
            lnb_state["tiles"].append(v_last)

        emit_ln(AB, 2048, G4, lambda tix: ("v", ab_v[tix]),
                XSQ_B, (2, 3, 4, 5), lnb_cb)
        t3 = lnb_state["t3"]
        lnb_resid(*t3, chunks=(6, 7), final=False)
        fn, lnb_pend[0] = lnb_pend[0], None
        fn()
        lnb_resid(*t3, chunks=(0, 1, 2, 3, 4, 5), final=True)
        v_x1 = lnb_state["v"]
        g_x1 = lnb_state["g"]
        v_x1_tiles = lnb_state["tiles"]

        v_x2 = 0
        g_x2 = 0
        t_down_prev = 0
        o_prev_dma = 0
        nout = [0]
        silu_guard = {}
        h_guard = {}
        gu_mm_guard = {}
        lnd_pend = [None]
        pre_cast = [0]

        def emit_lnc_casts(half_, hbase_, tdp):
            last = 0
            for c in range(CH):
                v_need = (max(v_x1_tiles[1], lnb_state["v67"]) if half_ == 0
                          else v_x1)
                w = [("v", v_need), ("g", g_x1), ("t", tdp),
                     ("t", gd["xsq_t"]), ("q", wb_gu[(5, "u")])]

                def fn(e, c=c, hbase_=hbase_):
                    return e.activation(v16(LNC + c * 2 * 1024, 0, P, 1024),
                                        v32(AO + c * 4 * 2048 + 4 * hbase_, 0, P, 1024),
                                        AF.Copy)
                last = s.emit("scalar", fn, waits=w, inc="a")
            return last
        pending_od = [None]

        def flush_od(eng="sync"):
            if pending_od[0] is not None:
                ods, pending_od[0] = pending_od[0], None
                for fn, w in ods:
                    s.emit(eng, fn, waits=w, inc="o", inc_by=16)
        prev_lnd_evB = 0
        for half in range(2):
            flush_od()
            hbase = half * 1024
            if half == 0:
                a_lnc_cast = emit_lnc_casts(0, 0, t_down_prev)
            else:
                a_lnc_cast = pre_cast[0]
            gridh = [(0, 512), (512, 512)]
            lnc_state = {"v": 0, "tiles": []}

            lnc_pend = [None]

            def lnc_cb(td):
                if lnd_pend[0] is not None:
                    fn, lnd_pend[0] = lnd_pend[0], None
                    fn()
                if lnc_pend[0] is not None:
                    fn, lnc_pend[0] = lnc_pend[0], None
                    fn()
                t0, W = td["t0"], td["W"]
                a_evR, a_evB = emit_rb_evac(td, None, as_f32=False)
                rs_, bs_ = td["rslot"], td["bslot"]
                lnc_pend[0] = lambda: lnc_norm(td, t0, W, rs_, bs_, a_evR, a_evB)

            def lnc_norm(td, t0, W, rs_, bs_, a_evR, a_evB):
                v_last = 0
                for c in range(CH):
                    def fn_n1(e, c=c, t0=t0, W=W, rs_=rs_):
                        return e.tensor_tensor(v16(NSCF, 0, P, W),
                                               v16(LNC + c * 2 * 1024 + 2 * t0, 0, P, W),
                                               v16(rs_, 0, P, W), ALU.mult)
                    s.emit("vector", fn_n1,
                           waits=[("a", a_evR), ("a", a_evB), ("g", s.val("g"))], inc="v")

                    def fn_n2(e, c=c, t0=t0, W=W, bs_=bs_):
                        return e.tensor_tensor(v16(LNC + c * 2 * 1024 + 2 * t0, 0, P, W),
                                               v16(NSCF, 0, P, W),
                                               v16(bs_, 0, P, W), ALU.add)
                    v_last = s.emit("vector", fn_n2, inc="v")
                gd[f"rbt_v{td['par']}"] = v_last
                lnc_state["v"] = v_last
                lnc_state["tiles"].append(v_last)

            emit_ln(LNC, 1024, gridh, lambda tix: ("a", a_lnc_cast), XSQ_F, (0, 1, 4, 5), lnc_cb)
            fn, lnc_pend[0] = lnc_pend[0], None
            fn()
            flush_od()
            v_lnc = lnc_state["v"]
            v_lnc_tiles = lnc_state["tiles"]

            v_h = 0
            w_slab = {}
            for sb in range(6):
                for kind, srcw, scr, bfo in (("g", wg, gscr, GBF), ("u", wu, uscr, UBF)):
                    w = []
                    if sb >= 2 and (sb - 2, kind) in gu_mm_guard:
                        w.append(("t", gu_mm_guard[(sb - 2, kind)]))
                    if sb == 0:
                        w += [("t", s.val("t")), ("a", s.val("a"))]
                    elif sb == 1:
                        w += [("v", s.val("v")), ("a", s.val("a")), ("g", s.val("g")),
                              ("o", o_prev_dma), ("t", s.val("t"))]
                    bslot = bfo + (sb % 2) * 8_192
                    w.append(("q", wb_gu[(sb, kind)]))
                    if kind == "g":
                        cnt = sdma(
                            lambda bslot=bslot: v16(bslot, 0, P, 4096),
                            lambda scr=scr, sb=sb: scr[:, 4096 * sb:4096 * sb + 4096],
                            waits=w)
                        w_slab[(sb, kind)] = ("q", cnt)
                    else:
                        cnt = wdma(
                            lambda bslot=bslot: v16(bslot, 0, P, 4096),
                            lambda scr=scr, sb=sb: scr[:, 4096 * sb:4096 * sb + 4096],
                            waits=w)
                        w_slab[(sb, kind)] = ("w", cnt)
                for hq in range(4):
                    hc = 4 * sb + hq
                    for tt in range(2):
                        t0 = tt * 512
                        gbank, ubank = tt, 2 + tt

                        def fn_gmm(e, sb=sb, hq=hq, t0=t0, gbank=gbank):
                            ins = None
                            for ic in range(CH):
                                ins = e.matmul(pb[gbank][0:P, 0:512],
                                               v16(GBF + (sb % 2) * 8_192 + hq * 2_048
                                                   + 2 * ic * 128, 0, P, 128),
                                               v16(LNC + ic * 2 * 1024 + 2 * t0, 0, P, 512),
                                               start=(ic == 0), stop=(ic == CH - 1))
                            return ins
                        wg_ = [w_slab[(sb, "g")], ("v", v_lnc_tiles[tt])]
                        if (hc - 2, tt) in silu_guard:
                            wg_.append(("a", silu_guard[(hc - 2, tt)]))
                        elif hc <= 1:
                            wg_.append(("a", s.val("a")))
                        t_g = s.emit("tensor", fn_gmm, waits=wg_, inc="t")
                        if hq == 3 and tt == 1:
                            gu_mm_guard[(sb, "g")] = t_g

                        def fn_umm(e, sb=sb, hq=hq, t0=t0, ubank=ubank):
                            ins = None
                            for ic in range(CH):
                                ins = e.matmul(pb[ubank][0:P, 0:512],
                                               v16(UBF + (sb % 2) * 8_192 + hq * 2_048
                                                   + 2 * ic * 128, 0, P, 128),
                                               v16(LNC + ic * 2 * 1024 + 2 * t0, 0, P, 512),
                                               start=(ic == 0), stop=(ic == CH - 1))
                            return ins
                        wu_ = [w_slab[(sb, "u")]]
                        if (hc - 2, tt) in h_guard:
                            wu_.append(("v", h_guard[(hc - 2, tt)]))
                        elif hc <= 1:
                            wu_.append(("v", s.val("v")))
                        t_u = s.emit("tensor", fn_umm, waits=wu_, inc="t")
                        if hq == 3 and tt == 1:
                            gu_mm_guard[(sb, "u")] = t_u

                        sg_slot = SGT + tt * 1_024

                        def fn_silu(e, gbank=gbank, sg_slot=sg_slot):
                            return e.activation(v16(sg_slot, 0, P, 512), pb[gbank][0:P, 0:512],
                                                AF.Silu, scale=DEQ)
                        ws_ = [("t", t_g)]
                        if hc == 0:
                            ws_.append(("g", s.val("g")))
                        a_silu = s.emit("scalar", fn_silu, waits=ws_, inc="a")
                        silu_guard[(hc, tt)] = a_silu

                        def fn_h(e, hc=hc, t0=t0, ubank=ubank, sg_slot=sg_slot):
                            return e.tensor_tensor(v16(HBUF + hc * 2_048 + 2 * t0, 0, P, 512),
                                                   v16(sg_slot, 0, P, 512),
                                                   pb[ubank][0:P, 0:512], ALU.mult)
                        v_h = s.emit("vector", fn_h, waits=[("a", a_silu), ("t", t_u)], inc="v")
                        h_guard[(hc, tt)] = v_h

            wd_slot_guard = [0, 0, 0, 0]
            for tt in range(2):
                t0 = tt * 512
                t_wd_mm = {}
                w_wd = {}
                a_down_last = 0
                oc_order = [2, 3, 4, 5, 6, 7, 0, 1]
                def emit_wd_oc(oci, extra_t):
                    oc = oc_order[oci]
                    slot = WDBF + (oci % 4) * 6_144
                    w = [("v", v_h), ("t", extra_t), ("q", wb_wd[oci // 2])]
                    if oci % 2 == 0:
                        cnt = wdma(lambda slot=slot: v16(slot, 0, P, 3072),
                                   lambda oc=oc: dscr[:, 3072 * oc:3072 * oc + 3072],
                                   waits=w)
                        w_wd[oci] = [("w", cnt)]
                    else:
                        cnt = sdma(lambda slot=slot: v16(slot, 0, P, 3072),
                                   lambda oc=oc: dscr[:, 3072 * oc:3072 * oc + 3072],
                                   waits=w)
                        w_wd[oci] = [("q", cnt)]

                for _oci in range(4):
                    emit_wd_oc(_oci, wd_slot_guard[_oci % 4] or s.val("t"))
                flush_od("scalar")
                for oci, oc in enumerate(oc_order):
                    wdslot = WDBF + (oci % 4) * 6_144

                    def fn_dmm(e, t0=t0, oc=oc, wdslot=wdslot):
                        ins = None
                        for hcc in range(HC):
                            ins = e.matmul(pb[oc][0:P, 0:512],
                                           v16(wdslot + 2 * hcc * 128, 0, P, 128),
                                           v16(HBUF + hcc * 2_048 + 2 * t0, 0, P, 512),
                                           start=(hcc == 0), stop=(hcc == HC - 1))
                        return ins
                    wm = [*w_wd[oci], ("v", v_h)]
                    if oc in (0, 1):
                        wm.append(("a", prev_lnd_evB))
                        wm.append(("a", s.val("a")))
                    t_dmm = s.emit("tensor", fn_dmm, waits=wm, inc="t")
                    t_wd_mm[oci] = t_dmm
                    wd_slot_guard[oci % 4] = t_dmm
                    if oci + 4 <= 7:
                        emit_wd_oc(oci + 4, t_dmm)

                    def fn_dev(e, oc=oc):
                        e.activation(v32(FFO + oc * 4 * 512, 0, P, 512),
                                     pb[oc][0:P, 0:512], AF.Identity, scale=DEQ * DEQ)
                        e.activation(v16(FBT + oc * 2 * 512, 0, P, 512),
                                     pb[oc][0:P, 0:512], AF.Identity, scale=DEQ * DEQ)
                        return e.activation(v16(XSQ_F + oc * 2 * 512, 0, P, 512),
                                            pb[oc][0:P, 0:512], AF.Square, scale=DEQ * DEQ)
                    a_down_last = s.emit("scalar", fn_dev,
                                         waits=[("t", t_dmm), ("o", o_prev_dma)], inc="a")

                if half == 0 and tt == 1:
                    pre_cast[0] = emit_lnc_casts(1, 1024, t_wd_mm[7])

                lnd_state = {"v": 0, "evB": 0}

                def lnd_cb(td):
                    a_evR, a_evB = emit_rb_evac(td, None, as_f32=True)
                    lnd_state["evR"] = a_evR
                    lnd_state["evB"] = a_evB

                lnd_tiles = emit_ln(FBT, 512, [(0, 512)], lambda tix: ("a", a_down_last),
                                    XSQ_F, (0, 1), lnd_cb, skip_square=True)
                td = lnd_tiles[0]
                a_evR, a_evB = lnd_state["evR"], lnd_state["evB"]
                prev_lnd_evB = a_evB

                def lnd_resid(t0=t0, hbase=hbase, a_evR=a_evR, a_evB=a_evB):
                    v_x2 = 0
                    g_x2 = 0
                    pair_done = [0, 0, 0, 0]
                    for c in range(CH):
                        pool = c >= CH - NPOOL
                        eng, sem = ("gpsimd", "g") if pool else ("vector", "v")
                        sc1, sc2 = (NSCF, SGT) if pool else (T12_F, T12_F + 2_048)

                        def fn_r1(e, c=c, sc1=sc1):
                            return e.tensor_tensor(v32(sc1, 0, P, 512),
                                                   v32(FFO + c * 4 * 512, 0, P, 512),
                                                   v32(RBT, 0, P, 512), ALU.mult)
                        s.emit(eng, fn_r1, waits=[("a", a_evR), ("a", a_evB)], inc=sem)

                        def fn_r2(e, sc1=sc1, sc2=sc2):
                            return e.tensor_tensor(v32(sc2, 0, P, 512), v32(sc1, 0, P, 512),
                                                   v32(RBT + 2_080, 0, P, 512), ALU.add)
                        s.emit(eng, fn_r2, inc=sem)

                        def fn_r3(e, c=c, t0=t0, hbase=hbase, sc2=sc2):
                            return e.tensor_tensor(v32(FFO + c * 4 * 512, 0, P, 512),
                                                   v32(sc2, 0, P, 512),
                                                   v32(AO + c * 4 * 2048 + 4 * (hbase + t0), 0, P, 512),
                                                   ALU.add)
                        if pool:
                            g_x2 = s.emit(eng, fn_r3, inc=sem)
                        else:
                            v_x2 = s.emit(eng, fn_r3, inc=sem)
                        if c % 2 == 1 and not pool:
                            pair_done[c // 2] = v_x2
                    gd["rbt_v0"] = v_x2
                    gd["rbt_g0"] = g_x2

                    ods = []
                    for k in range(4):
                        def fn_od(e, t0=t0, hbase=hbase, k=k):
                            return e.dma_start(
                                out=out[:, 2 * k:2 * k + 2, hbase + t0: hbase + t0 + 512],
                                in_=v32(FFO + k * 2 * 4 * 512, 0, P, 2 * 512).rearrange(
                                    "p (c m) -> p c m", c=2))
                        w = [("v", pair_done[k])]
                        if k == 3:
                            w.append(("g", g_x2))
                        ods.append((fn_od, w))
                    pending_od[0] = ods

                nout[0] += 4
                o_prev_dma = 16 * nout[0]
                if tt == 0:
                    lnd_resid()
                else:
                    lnd_pend[0] = lnd_resid
                t_down_prev = t_wd_mm[7]

        if lnd_pend[0] is not None:
            fn, lnd_pend[0] = lnd_pend[0], None
            fn()
        flush_od()
        if _kdump is not None:
            s.trunc = None
            doff = int(_kdump)
            allw = [(k, s.val(k)) for k in ("t", "a", "v", "g")] + \
                   [("d", s.val("d")), ("w", s.val("w")), ("o", s.val("o"))]

            def fn_dump(e, doff=doff):
                return e.dma_start(out=out[:],
                                   in_=v32(doff, 0, P, CH * 2048).rearrange(
                                       "p (c m) -> p c m", c=CH))
            s.emit("sync", fn_dump, waits=allw, inc="o", inc_by=16)
            nout[0] += 1
        s.emit("sync", lambda e: e.nop(), waits=[("o", 16 * nout[0])])

        @block.sync
        def _(sync):
            s.run("sync", sync, sems)

        @block.tensor
        def _(tensor):
            s.run("tensor", tensor, sems)

        @block.scalar
        def _(scalar):
            s.run("scalar", scalar, sems)

        @block.vector
        def _(vector):
            s.run("vector", vector, sems)

        @block.gpsimd
        def _(gpsimd):
            s.run("gpsimd", gpsimd, sems)

    return nc


_BF = ml_dtypes.bfloat16


def _tile_weight(w):
    wt = np.ascontiguousarray(w.T)
    return np.ascontiguousarray(wt.reshape(CH, P, w.shape[0]).transpose(1, 0, 2))


def _tile_gu(wint):
    wt = np.ascontiguousarray(wint.T)
    a = wt.reshape(CH, P, HC, P).transpose(2, 1, 0, 3)
    return np.ascontiguousarray(a.reshape(HC, P, CH * P))


def _tile_wd(wint):
    wt = np.ascontiguousarray(wint.T)
    a = wt.reshape(HC, P, CH, P).transpose(2, 1, 0, 3)
    return np.ascontiguousarray(a.reshape(CH, P, HC * P))


def kernel(L0_attn_x, L0_attn_Wp, L0_attn_bp, L0_attn_Wq, L0_attn_bq, L0_attn_Wk,
           L0_attn_bk, L0_attn_Wv, L0_attn_bv, L0_attn_Wo, L0_attn_bo,
           L0_ffn_w_gate, L0_ffn_w_up, L0_ffn_w_down):
    x = np.asarray(L0_attn_x, dtype=np.float32)
    S = x.shape[1]
    xt = np.ascontiguousarray(x[0].T)

    wp = _tile_weight(np.asarray(L0_attn_Wp, np.float32))
    wq = _tile_weight(np.asarray(L0_attn_Wq, np.float32))
    wk = _tile_weight(np.asarray(L0_attn_Wk, np.float32))
    wv = _tile_weight(np.asarray(L0_attn_Wv, np.float32))
    wo = _tile_weight(np.asarray(L0_attn_Wo, np.float32))
    wg_t = _tile_gu(np.asarray(L0_ffn_w_gate, np.int32))
    wu_t = _tile_gu(np.asarray(L0_ffn_w_up, np.int32))
    wd_t = _tile_wd(np.asarray(L0_ffn_w_down, np.int32))

    bias_all = np.zeros((P, CH * 5), np.float32)
    for wi_, b in enumerate((L0_attn_bp, L0_attn_bq, L0_attn_bk, L0_attn_bv, L0_attn_bo)):
        b = np.asarray(b, np.float32).reshape(CH, P)
        for oc in range(CH):
            bias_all[:, oc * 5 + wi_] = b[oc]

    e16 = np.zeros((P, CH * H), np.float32)
    for c in range(CH):
        for pp_ in range(P):
            e16[pp_, c * H + 2 * c + (pp_ // 64)] = 1.0
    eb = np.zeros((H, CH * P), np.float32)
    for c in range(CH):
        for pp_ in range(P):
            eb[2 * c + (pp_ // 64), c * P + pp_] = 1.0

    nc = build_program()
    in_maps = []
    for core in range(NCORE):
        start = core * T
        idx = (np.arange(start - 1, start + T + 1) % S)
        xc = np.ascontiguousarray(xt[:, idx])
        x_tiled = np.ascontiguousarray(xc.reshape(CH, P, THALO).transpose(1, 0, 2))
        in_maps.append(dict(
            x_in=x_tiled, wp=wp, wk=wk, wv=wv, wq=wq, wo=wo,
            biases=bias_all, e16=e16.astype(_BF), ebc=eb.astype(_BF),
            wg=wg_t, wu=wu_t, wdt=wd_t,
        ))
    res = run_bass_kernel_spmd(nc, in_maps, core_ids=list(range(NCORE)))
    shards = []
    for core in range(NCORE):
        o = np.asarray(res.results[core]["out"])
        shards.append(o.transpose(1, 0, 2).reshape(D, T))
    y = np.concatenate(shards, axis=1)
    return np.ascontiguousarray(y.T)[None, :, :]


# version 66
# speedup vs baseline: 1.0026x; 1.0026x over previous
import numpy as np
import ml_dtypes
from contextlib import ExitStack

import concourse.bass as bass
import concourse.mybir as mybir
from concourse.bass_utils import run_bass_kernel_spmd

F32 = mybir.dt.float32
BF16 = mybir.dt.bfloat16
I32 = mybir.dt.int32
U8 = mybir.dt.uint8
AF = mybir.ActivationFunctionType
ALU = mybir.AluOpType

P = 128
CH = 8
D = 1024
T = 2048
THALO = 2050
H = 16
NCORE = 8
HID = 3072
HC = 24
SCALE_Q = 0.125
DEQ = 1.0 / 127.0
EPS = 1e-5

G5 = [(i * 410, 410) for i in range(5)]
G4 = [(i * 512, 512) for i in range(4)]

AR_BYTES = 212_832
RA = 0
RB = 65_664
RW = 131_456
RS = 180_608

XB = RA
XP = RA + 32_832
AO = RA

X0H = RB
KB = RB
VB = RB + 32_896
WOSTG = RB
X0R = RB
HBUF = RB
XSQ_F = RB + 49_152
T12_F = RB + 57_344
NSCF = RB + 61_440
SGT = RB + 63_520

STG = RW
WB0 = RW + 16_384
WB1 = RW + 32_768
WVS = STG
SIG = XB
SC123 = XB + 6 * 1_664
QSB = XB + 14_976
TMP = XB + 21_632
WFE = XB + 28_288
AB = RW
XSQ_B = RW + 32_768
T12_B = RW + 40_960
LNC = RW
GBF = RW + 16_384
UBF = RW + 32_768
WDBF = RW
FFO = RW + 24_576
FBT = RW + 40_960

ST_SL = RS
RBMR = ST_SL + 6 * 2_080
RBT = RBMR + 2 * 1_040
WN = RBT + 4 * 2_080
CONST_E16 = WN + 6 * 832
CONST_EB = CONST_E16 + 256
CONST_BIAS = CONST_EB + 2_048
NSCA = CONST_BIAS + 160
ONESC = NSCA + 832
ROWS1 = ONESC + 32
EPSC = ROWS1 + 512
RS_END = EPSC + 32
assert RS_END <= AR_BYTES, RS_END

T12_P = RW + 45_056

KW = 2056


class Sched:
    ENG = ("sync", "tensor", "scalar", "vector", "gpsimd")

    def __init__(self):
        self.q = {e: [] for e in self.ENG}
        self.cnt = {}
        self.last_wait = {}
        self.nemit = 0
        self.trunc = None

    def val(self, sem):
        return self.cnt.get(sem, 0)

    def emit(self, eng, fn, waits=(), inc=None, inc_by=1):
        self.nemit += 1
        if self.trunc is not None and self.nemit > self.trunc:
            return self.cnt.get(inc, 0) + inc_by if inc is not None else None
        newv = None
        if inc is not None:
            self.cnt[inc] = self.cnt.get(inc, 0) + inc_by
            newv = self.cnt[inc]
        w = []
        for sn, sv in waits:
            if not sv:
                continue
            key = (eng, sn)
            if self.last_wait.get(key, -1) >= sv:
                continue
            self.last_wait[key] = sv
            w.append((sn, sv))
        self.q[eng].append((fn, tuple(w), inc, inc_by))
        return newv

    def run(self, eng_name, engine, sems):
        for fn, waits, inc, inc_by in self.q[eng_name]:
            for sn, sv in waits:
                engine.wait_ge(sems[sn], sv)
            ins = fn(engine)
            if inc is not None:
                ins.then_inc(sems[inc], inc_by)


def build_program():
    import os
    _ktrunc = os.environ.get("KTRUNC")
    _kdump = os.environ.get("KDUMP")
    nc = bass.Bass("TRN2", target_bir_lowering=False, debug=False)

    x_in = nc.dram_tensor("x_in", [P, CH, THALO], F32, kind="ExternalInput")
    wts = {w: nc.dram_tensor(w, [P, CH, D], F32, kind="ExternalInput")
           for w in ("wp", "wk", "wv", "wq", "wo")}
    biases = nc.dram_tensor("biases", [P, CH * 5], F32, kind="ExternalInput")
    e16 = nc.dram_tensor("e16", [P, CH * H], BF16, kind="ExternalInput")
    ebc = nc.dram_tensor("ebc", [H, CH * P], BF16, kind="ExternalInput")
    wg = nc.dram_tensor("wg", [HC, P, CH * P], I32, kind="ExternalInput")
    wu = nc.dram_tensor("wu", [HC, P, CH * P], I32, kind="ExternalInput")
    wdt = nc.dram_tensor("wdt", [CH, P, HC * P], I32, kind="ExternalInput")
    out = nc.dram_tensor("out", [P, CH, T], F32, kind="ExternalOutput")
    gscr = nc.dram_tensor("gscr", [P, HC * 1024], BF16, kind="Internal")
    uscr = nc.dram_tensor("uscr", [P, HC * 1024], BF16, kind="Internal")
    dscr = nc.dram_tensor("dscr", [P, CH * 3072], BF16, kind="Internal")

    s = Sched()
    if _ktrunc:
        s.trunc = int(_ktrunc)

    with ExitStack() as st:
        en = st.enter_context
        arena = en(nc.sbuf_tensor("arena", [P, AR_BYTES], U8))
        a16 = arena.bitcast(BF16)
        a32 = arena.bitcast(F32)
        ai32 = arena.bitcast(I32)
        pb = [en(nc.psum_tensor(f"pb{i}", [P, 512], F32)) for i in range(8)]
        sems = {n: en(nc.semaphore(n)) for n in ("d", "t", "a", "v", "g", "o", "w", "q")}
        block = en(nc.Block())

        def v16(off, p0, np_, n):
            assert off % 2 == 0 and off + 2 * n <= AR_BYTES
            return a16[p0:p0 + np_, off // 2: off // 2 + n]

        def v32(off, p0, np_, n):
            assert off % 4 == 0 and off + 4 * n <= AR_BYTES
            return a32[p0:p0 + np_, off // 4: off // 4 + n]

        def vi32(off, p0, np_, n):
            assert off % 4 == 0 and off + 4 * n <= AR_BYTES
            return ai32[p0:p0 + np_, off // 4: off // 4 + n]

        ndma = [0]

        def dma(dst_fn, src_fn, waits=()):
            ndma[0] += 1

            def fn(e):
                return e.dma_start(out=dst_fn(), in_=src_fn())
            s.emit("sync", fn, waits=waits, inc="d", inc_by=16)
            return 16 * ndma[0]

        nw = [0]

        def wdma(dst_fn, src_fn, waits=()):
            nw[0] += 1

            def fn(e):
                return e.dma_start(out=dst_fn(), in_=src_fn())
            s.emit("gpsimd", fn, waits=waits, inc="w", inc_by=16)
            return 16 * nw[0]

        nq = [0]

        def sdma(dst_fn, src_fn, waits=()):
            nq[0] += 1

            def fn(e):
                return e.dma_start(out=dst_fn(), in_=src_fn())
            s.emit("sync", fn, waits=waits, inc="q", inc_by=16)
            return 16 * nq[0]

        w_xb1 = wdma(lambda: v16(XB, 0, P, CH * 2052).rearrange("p (c m) -> p c m", c=CH)[:, :, 0:412],
                     lambda: x_in[:, :, 0:412])
        w_xb = wdma(lambda: v16(XB, 0, P, CH * 2052).rearrange("p (c m) -> p c m", c=CH)[:, :, 412:THALO],
                    lambda: x_in[:, :, 412:THALO])

        d_e16 = dma(lambda: v16(CONST_E16, 0, P, CH * H), lambda: e16[:])
        d_eb = dma(lambda: v16(CONST_EB, 0, H, CH * P), lambda: ebc[:])
        d_bias = dma(lambda: v32(CONST_BIAS, 0, P, CH * 5), lambda: biases[:])

        def fn_mc(e):
            e.memset(v16(ONESC, 0, P, 1), 1.0)
            e.memset(v16(ROWS1, 0, 1, P), 1.0)
            e.memset(v16(ROWS1, 32, 1, P), 1.0)
            e.memset(v32(EPSC, 0, 1, 1), EPS)
            e.memset(v16(ROWS1 + 256, 32, 1, P), -1.0)
            return e.memset(v16(ROWS1 + 256, 0, 1, P), -1.0)
        v_consts = s.emit("vector", fn_mc, inc="v")

        def fn_warm(e):
            return e.activation(v16(NSCA, 0, 1, 1), v16(ONESC, 0, 1, 1), AF.Square)
        s.emit("scalar", fn_warm, waits=[("v", v_consts)], inc="a")

        def load_attn_weight(name, wbuf_off, pe_free=0):
            w = [("t", pe_free)] if pe_free else []
            return wdma(lambda: v16(wbuf_off, 0, P, 8192).rearrange("p (c m) -> p c m", c=CH),
                        lambda: wts[name][:], waits=w)

        w_wp = load_attn_weight("wp", WB0)
        w_wk = load_attn_weight("wk", WB1)

        gd = {"rbt_v0": 0, "rbt_v1": 0, "rbt_g0": 0, "rbt_g1": 0, "xsq_t": 0,
              "stats_v0": 0, "stats_v1": 0, "rb_t0": 0, "rb_t1": 0, "mr_t0": 0, "mr_t1": 0}
        for _b in range(8):
            gd[f"bankev_{_b}"] = 0
            gd[f"bankev_v{_b}"] = 0

        def emit_ln(src_off, src_stride_el, grid, wait_src_fn, xsq_off, sums_banks, tile_cb,
                    n_chunks=8, skip_square=False, sq_dve=False):
            tiles = []
            for tix, (t0, W) in enumerate(grid):
                par = tix % 2
                pbase = 32 * par
                if len(sums_banks) == 4:
                    bs, bq = sums_banks[2 * par], sums_banks[2 * par + 1]
                else:
                    bs, bq = sums_banks
                wait_src = wait_src_fn(tix)
                sq_last = None
                sq_sem = "v" if sq_dve else "a"
                if not skip_square:
                    for c in range(n_chunks):
                        if sq_dve:
                            def fn(e, c=c, t0=t0, W=W):
                                src_ap = v16(src_off + c * 2 * src_stride_el + 2 * t0, 0, P, W)
                                return e.tensor_tensor(v16(xsq_off + c * 2 * 512, 0, P, W),
                                                       src_ap, src_ap, ALU.mult)
                            sq_last = s.emit("vector", fn,
                                             waits=[wait_src, ("t", gd["xsq_t"])], inc="v")
                        else:
                            def fn(e, c=c, t0=t0, W=W):
                                return e.activation(v16(xsq_off + c * 2 * 512, 0, P, W),
                                                    v16(src_off + c * 2 * src_stride_el + 2 * t0,
                                                        0, P, W), AF.Square)
                            sq_last = s.emit("scalar", fn,
                                             waits=[wait_src, ("t", gd["xsq_t"])], inc="a")

                def fn_sum(e, t0=t0, W=W, bs=bs):
                    ins = None
                    for c in range(n_chunks):
                        ins = e.matmul(pb[bs][0:1, 0:W], v16(ONESC, 0, P, 1),
                                       v16(src_off + c * 2 * src_stride_el + 2 * t0, 0, P, W),
                                       start=(c == 0), stop=(c == n_chunks - 1))
                    return ins
                t_sum = s.emit("tensor", fn_sum,
                               waits=[wait_src, ("a", gd[f"bankev_{bs}"]),
                                      ("v", gd[f"bankev_v{bs}"]), ("v", v_consts)],
                               inc="t")

                def fn_ssq(e, W=W, bq=bq):
                    ins = None
                    for c in range(n_chunks):
                        ins = e.matmul(pb[bq][0:1, 0:W], v16(ONESC, 0, P, 1),
                                       v16(xsq_off + c * 2 * 512, 0, P, W),
                                       start=(c == 0), stop=(c == n_chunks - 1))
                    return ins
                t_ssq = s.emit("tensor", fn_ssq,
                               waits=[(sq_sem, sq_last if sq_last else 0), wait_src,
                                      ("a", gd[f"bankev_{bq}"]),
                                      ("v", gd[f"bankev_v{bq}"])], inc="t")
                gd["xsq_t"] = t_ssq

                m_of, nm_of, vv_of = (ST_SL + 2 * 2_080, ST_SL + 3 * 2_080, ST_SL + 4 * 2_080)
                inv_d = 1.0 / D

                def fn_m(e, W=W, bs=bs, pbase=pbase):
                    return e.tensor_scalar_mul(v32(m_of, pbase, 1, W), pb[bs][0:1, 0:W], inv_d)
                v_m = s.emit("vector", fn_m,
                             waits=[("t", t_sum), ("v", gd[f"stats_v{par}"])], inc="v")
                gd[f"bankev_v{bs}"] = v_m

                def fn_nm(e, W=W, pbase=pbase):
                    return e.scalar_tensor_tensor(v32(nm_of, pbase, 1, W), v32(m_of, pbase, 1, W),
                                                  -1.0, v32(m_of, pbase, 1, W),
                                                  ALU.mult, ALU.mult)
                s.emit("vector", fn_nm, inc="v")

                def fn_vv(e, W=W, bq=bq, pbase=pbase):
                    return e.scalar_tensor_tensor(v32(vv_of, pbase, 1, W),
                                                  pb[bq][0:1, 0:W], inv_d,
                                                  v32(nm_of, pbase, 1, W), ALU.mult, ALU.add)
                v_vv = s.emit("vector", fn_vv, waits=[("t", t_ssq)], inc="v")
                gd[f"bankev_v{bq}"] = v_vv

                def fn_ln_(e, W=W, pbase=pbase):
                    return e.activation(v32(vv_of, pbase, 1, W), v32(vv_of, pbase, 1, W), AF.Ln,
                                        bias=v32(EPSC, 0, 1, 1))
                s.emit("scalar", fn_ln_, waits=[("v", v_vv), ("v", v_consts)], inc="a")

                def fn_exp(e, W=W, pbase=pbase):
                    return e.activation(v16(RBMR, pbase, 1, W), v32(vv_of, pbase, 1, W),
                                        AF.Exp, scale=-0.5)
                a_rb = s.emit("scalar", fn_exp, waits=[("t", gd[f"rb_t{par}"])], inc="a")

                def fn_mr(e, W=W, pbase=pbase):
                    return e.tensor_tensor(v16(RBMR + 1_040, pbase, 1, W), v32(m_of, pbase, 1, W),
                                           v16(RBMR, pbase, 1, W), ALU.mult)
                v_mr = s.emit("vector", fn_mr, waits=[("a", a_rb), ("t", gd[f"mr_t{par}"])], inc="v")
                gd[f"stats_v{par}"] = v_mr

                def fn_bR(e, W=W, bs=bs, pbase=pbase):
                    return e.matmul(pb[bs][:, 0:W], v16(ROWS1, pbase, 1, P),
                                    v16(RBMR, pbase, 1, W), start=True, stop=True)
                t_bR = s.emit("tensor", fn_bR, waits=[("a", a_rb), ("v", v_m)], inc="t")
                gd[f"rb_t{par}"] = t_bR

                def fn_bB(e, W=W, bq=bq, pbase=pbase):
                    return e.matmul(pb[bq][:, 0:W], v16(ROWS1 + 256, pbase, 1, P),
                                    v16(RBMR + 1_040, pbase, 1, W), start=True, stop=True)
                t_bB = s.emit("tensor", fn_bB, waits=[("v", v_mr)], inc="t")
                gd[f"mr_t{par}"] = t_bB
                td = dict(t0=t0, W=W, t_bR=t_bR, t_bB=t_bB, par=par, bs=bs, bq=bq)
                tile_cb(td)
                tiles.append(td)
            return tiles

        def emit_rb_evac(td, banks, as_f32):
            del banks
            W = td["W"]
            par = td.get("par", 0)
            vv = v32 if as_f32 else v16
            rslot = RBT + par * 4_160
            bslot = RBT + par * 4_160 + 2_080

            def fn_evR(e, td=td, W=W, rslot=rslot):
                return e.activation(vv(rslot, 0, P, W), pb[td["bs"]][:, 0:W], AF.Copy)
            a_evR = s.emit("scalar", fn_evR,
                           waits=[("t", td["t_bR"]), ("v", gd[f"rbt_v{par}"]),
                                  ("g", gd[f"rbt_g{par}"])], inc="a")
            gd[f"bankev_{td['bs']}"] = a_evR

            def fn_evB(e, td=td, W=W, bslot=bslot):
                return e.activation(vv(bslot, 0, P, W), pb[td["bq"]][:, 0:W], AF.Copy)
            a_evB = s.emit("scalar", fn_evB, waits=[("t", td["t_bB"])], inc="a")
            gd[f"bankev_{td['bq']}"] = a_evB
            td["rslot"] = rslot
            td["bslot"] = bslot
            return a_evR, a_evB

        lna_state = {"v": 0}
        lna_pend = [None]

        def lna_cb(td):
            if lna_pend[0] is not None:
                fn, lna_pend[0] = lna_pend[0], None
                fn()
            t0, W = td["t0"], td["W"]
            a_evR, a_evB = emit_rb_evac(td, None, as_f32=False)
            rs_, bs_ = td["rslot"], td["bslot"]
            lna_pend[0] = lambda: lna_norm(td, t0, W, rs_, bs_, a_evR, a_evB)

        def lna_norm(td, t0, W, rs_, bs_, a_evR, a_evB):
            v_last = 0
            for c in range(CH):
                def fn_n1(e, c=c, t0=t0, W=W, rs_=rs_):
                    return e.tensor_tensor(v16(NSCA, 0, P, W),
                                           v16(XB + c * 2 * 2052 + 2 * t0, 0, P, W),
                                           v16(rs_, 0, P, W), ALU.mult)
                s.emit("vector", fn_n1, waits=[("a", a_evR), ("a", a_evB)], inc="v")

                def fn_n2(e, c=c, t0=t0, W=W, bs_=bs_):
                    return e.tensor_tensor(v16(XB + c * 2 * 2052 + 2 * t0, 0, P, W),
                                           v16(NSCA, 0, P, W),
                                           v16(bs_, 0, P, W), ALU.add)
                v_last = s.emit("vector", fn_n2, inc="v")
            td["v_norm"] = v_last
            gd[f"rbt_v{td['par']}"] = v_last
            lna_state["v"] = v_last

        lna_tiles = emit_ln(XB, 2052, G5,
                            lambda tix: ("w", w_xb1 if tix == 0 else w_xb),
                            XP, (6, 7, 4, 5), lna_cb)
        fn, lna_pend[0] = lna_pend[0], None
        fn()
        v_lna = lna_state["v"]

        def emit_proj_tile(state, wbuf_off, rhs_fn, dst_fn, ti, t0, W, tile_waits,
                           mm_extra=(), ev_extra=(), banks=(0, 1)):
            a_evs = state["a_evs"]
            for oc in range(CH):
                bank = banks[len(a_evs) % 2]
                waits = list(tile_waits) + list(mm_extra)
                if len(a_evs) >= 2:
                    waits.append(("a", a_evs[-2]))
                else:
                    waits.append(("a", s.val("a")))

                def fn_mm(e, oc=oc, t0=t0, W=W, bank=bank):
                    ins = None
                    for ic in range(CH):
                        ins = e.matmul(pb[bank][0:P, 0:W],
                                       v16(wbuf_off + 2 * (ic * 1024 + oc * 128), 0, P, 128),
                                       rhs_fn(ic, t0, W),
                                       start=(ic == 0), stop=(ic == CH - 1))
                    return ins
                state["t_last"] = s.emit("tensor", fn_mm, waits=waits, inc="t")

                def fn_ev(e, oc=oc, t0=t0, W=W, bank=bank):
                    return dst_fn(e, oc, t0, W, pb[bank])
                a_evs.append(s.emit("scalar", fn_ev,
                                    waits=[("t", state["t_last"])] + list(ev_extra), inc="a"))

        def emit_proj(wbuf_off, bias_idx, rhs_fn, dst_fn, grid, tile_wait_fn,
                      mm_extra=(), ev_extra=(), banks=(0, 1)):
            state = {"a_evs": [], "t_last": 0}
            for ti, (t0, W) in enumerate(grid):
                emit_proj_tile(state, wbuf_off, rhs_fn, dst_fn, ti, t0, W, tile_wait_fn(ti),
                               mm_extra=mm_extra, ev_extra=ev_extra, banks=banks)
            return state["t_last"], state["a_evs"]

        def bias_ap(w_idx, oc):
            return v32(CONST_BIAS + 4 * (oc * 5 + w_idx), 0, P, 1)

        def rhs_lna(ic, t0, W):
            return v16(XB + ic * 2 * 2052 + 2 * t0, 0, P, W)

        def rhs_xp(ic, t0, W):
            return v16(XP + ic * 2 * 2052 + 2 * t0, 0, P, W)

        def dst_xp(e, oc, t0, W, ps):
            return e.activation(v16(XP + oc * 2 * 2052 + 2 * t0, 0, P, W), ps[0:P, 0:W],
                                AF.Identity, bias=bias_ap(0, oc))
        t_xp, a_xp_evs = emit_proj(WB0, 0, rhs_lna, dst_xp, G5,
                                   lambda ti: [("v", lna_tiles[ti]["v_norm"])],
                                   mm_extra=[("w", w_wp), ("d", d_bias)])

        w_wv = load_attn_weight("wv", WVS)

        def fn_guards(e):
            e.memset(a16[:, KB // 2: KB // 2 + CH * KW: KW], 0.0)
            e.memset(a16[:, KB // 2 + 2051: KB // 2 + CH * KW: KW], 0.0)
            e.memset(a16[:, VB // 2: VB // 2 + CH * KW: KW], 0.0)
            return e.memset(a16[:, VB // 2 + 2051: VB // 2 + CH * KW: KW], 0.0)
        v_guards = s.emit("vector", fn_guards, inc="v")

        def dst_k(e, oc, t0, W, ps):
            return e.activation(v16(KB + oc * 2 * KW + 2 * (t0 + 1), 0, P, W), ps[0:P, 0:W],
                                AF.Identity, bias=bias_ap(2, oc))

        def dst_v(e, oc, t0, W, ps):
            return e.activation(v16(VB + oc * 2 * KW + 2 * (t0 + 1), 0, P, W), ps[0:P, 0:W],
                                AF.Identity, bias=bias_ap(3, oc))

        w_wq = load_attn_weight("wq", WB0, pe_free=t_xp)

        ps_state = {"a_evs": [], "t_last": 0}

        v_subs_prev = 0
        v_win_done = 0
        win_done_at = []
        wfe_trio_guard = {5: 0, 2: 0}
        wn_guard = [0, 0]
        t_red_last = 0
        a_q_evs = ps_state["a_evs"]
        win_pp = 0
        pending_window = []
        t_k_last = 0
        t_v_last = 0
        qk_last = [0]

        def super_tiles():
            nonlocal t_k_last, t_v_last
            for ti, (t0, W) in enumerate(G5):
                emit_proj_tile(ps_state, WB1, rhs_xp, dst_k, ti, t0, W,
                               [("a", a_xp_evs[ti * CH + CH - 1])],
                               mm_extra=[("w", w_wk)], ev_extra=[("v", v_guards)])
                t_k_last = ps_state["t_last"]
                emit_proj_tile(ps_state, WVS, rhs_xp, dst_v, ti, t0, W,
                               [("a", a_xp_evs[ti * CH + CH - 1])],
                               mm_extra=[("w", w_wv)], ev_extra=[("v", v_guards)])
                t_v_last = ps_state["t_last"]
                if ti >= 1:
                    yield ti - 1
            yield len(G5) - 1

        for ti in super_tiles():
            t0, W = G5[ti]
            q_evs = []
            for c in range(CH):
                bank = len(a_q_evs) % 2
                waits = [("w", w_wq), ("a", a_xp_evs[ti * CH + CH - 1])]
                if len(a_q_evs) >= 2:
                    waits.append(("a", a_q_evs[-2]))
                else:
                    waits.append(("a", s.val("a")))

                def fn_qmm(e, c=c, t0=t0, W=W, bank=bank):
                    ins = None
                    for ic in range(CH):
                        ins = e.matmul(pb[bank][0:P, 0:W],
                                       v16(WB0 + 2 * (ic * 1024 + c * 128), 0, P, 128),
                                       rhs_xp(ic, t0, W),
                                       start=(ic == 0), stop=(ic == CH - 1))
                    return ins
                t_q = s.emit("tensor", fn_qmm, waits=waits, inc="t")

                def fn_qev(e, c=c, W=W, bank=bank):
                    return e.activation(v16(QSB + c * 832, 0, P, W), pb[bank][0:P, 0:W],
                                        AF.Identity, bias=bias_ap(1, c))
                a_q = s.emit("scalar", fn_qev, waits=[("t", t_q)], inc="a")
                q_evs.append(a_q)
                a_q_evs.append(a_q)

            for n in range(3):
                v_tm = 0
                for c in range(CH):
                    def fn_tm(e, c=c, n=n, t0=t0, W=W):
                        return e.tensor_tensor(v16(TMP + c * 832, 0, P, W),
                                               v16(QSB + c * 832, 0, P, W),
                                               v16(KB + c * 2 * KW + 2 * (t0 + n), 0, P, W),
                                               ALU.mult)
                    v_tm = s.emit("vector", fn_tm,
                                  waits=[("a", q_evs[c]), ("t", t_red_last)], inc="v")
                qk_last[0] = v_tm

                def fn_red(e, n=n, W=W):
                    ins = None
                    for c in range(CH):
                        ins = e.matmul(pb[2 + n][0:H, 0:W],
                                       v16(CONST_E16 + c * 2 * H, 0, P, H),
                                       v16(TMP + c * 832, 0, P, W),
                                       start=(c == 0), stop=(c == CH - 1))
                    return ins
                t_red_last = s.emit("tensor", fn_red,
                                    waits=[("v", v_tm), ("v", v_subs_prev), ("d", d_e16),
                                           ("v", wfe_trio_guard[2])], inc="t")

            SG = [SIG + i * 1_664 for i in range(6)]

            def fn_s0(e, W=W):
                return e.activation(v32(SG[0], 0, H, W), pb[2][0:H, 0:W], AF.Copy)
            a_s0 = s.emit("scalar", fn_s0, waits=[("t", t_red_last)], inc="a")

            def fn_mx1(e, W=W):
                return e.tensor_tensor(v32(SG[1], 0, H, W), v32(SG[0], 0, H, W),
                                       pb[3][0:H, 0:W], ALU.max)
            s.emit("vector", fn_mx1, waits=[("a", a_s0), ("t", t_red_last)], inc="v")

            def fn_mx2(e, W=W):
                return e.tensor_tensor(v32(SG[2], 0, H, W), v32(SG[1], 0, H, W),
                                       pb[4][0:H, 0:W], ALU.max)
            s.emit("vector", fn_mx2, inc="v")
            for n in range(3):
                def fn_sub(e, n=n, W=W):
                    return e.tensor_tensor(v32(SG[3 + n], 0, H, W), pb[2 + n][0:H, 0:W],
                                           v32(SG[2], 0, H, W), ALU.subtract)
                v_subs_prev = s.emit("vector", fn_sub, inc="v")
            if pending_window:
                v_win_done = pending_window.pop(0)(v_win_done, v_subs_prev)

            def fn_exp(e, W=W):
                return e.activation(v32(SG[3], 0, H, 3 * 416), v32(SG[3], 0, H, 3 * 416),
                                    AF.Exp, scale=SCALE_Q)
            a_exp = s.emit("scalar", fn_exp, waits=[("v", v_subs_prev)], inc="a")

            def fn_ad1(e, W=W):
                return e.tensor_tensor(v32(SG[0], 0, H, W), v32(SG[3], 0, H, W),
                                       v32(SG[4], 0, H, W), ALU.add)
            s.emit("vector", fn_ad1, waits=[("a", a_exp)], inc="v")

            def fn_ad2(e, W=W):
                return e.tensor_tensor(v32(SG[1], 0, H, W), v32(SG[0], 0, H, W),
                                       v32(SG[5], 0, H, W), ALU.add)
            s.emit("vector", fn_ad2, inc="v")

            def fn_rec(e, W=W):
                return e.reciprocal(v32(SG[0], 0, H, W), v32(SG[1], 0, H, W))
            s.emit("vector", fn_rec, inc="v")

            wslots = [WN + (win_pp * 3 + n) * 832 for n in range(3)]
            v_wn = 0
            for n in range(3):
                def fn_wn(e, n=n, W=W, wslots=wslots):
                    return e.tensor_tensor(v16(wslots[n], 0, H, W), v32(SG[3 + n], 0, H, W),
                                           v32(SG[0], 0, H, W), ALU.mult)
                v_wn = s.emit("vector", fn_wn, waits=[("t", wn_guard[win_pp])], inc="v")

            def emit_window(vw_in, subs_now, t0=t0, W=W, wslots=wslots, v_wn=v_wn,
                            win_pp=win_pp):
                vw = vw_in
                jlo = t0
                jhi = t0 + W
                WW = jhi - jlo
                t_wf = 0
                trio_v = dict(wfe_trio_guard)
                for c in range(CH):
                    b0 = 5 if c % 2 == 0 else 2
                    for n in range(3):
                        def fn_wf(e, c=c, n=n, b0=b0, jlo=jlo, WW=WW, wslots=wslots, t0=t0):
                            return e.matmul(pb[b0 + n][0:P, 0:WW],
                                            v16(CONST_EB + c * 2 * P, 0, H, P),
                                            v16(wslots[n] + 2 * (jlo - t0), 0, H, WW),
                                            start=True, stop=True)
                        wfw = [("v", v_wn), ("v", trio_v[b0]), ("d", d_eb)]
                        if b0 == 2:
                            wfw.append(("v", subs_now))
                        t_wf = s.emit("tensor", fn_wf, waits=wfw, inc="t")
                    SCa, SCb, SCc = SC123, SC123 + 1_664, SC123 + 2 * 1_664

                    def fn_w1(e, c=c, b0=b0, jlo=jlo, WW=WW):
                        return e.tensor_tensor(v32(SCa, 0, P, WW), pb[b0][0:P, 0:WW],
                                               v16(VB + c * 2 * KW + 2 * jlo, 0, P, WW), ALU.mult)
                    s.emit("vector", fn_w1,
                           waits=[("t", t_wf), ("a", a_q_evs[-1]), ("v", vw)], inc="v")

                    def fn_w2(e, c=c, b0=b0, jlo=jlo, WW=WW):
                        return e.tensor_tensor(v32(SCb, 0, P, WW), pb[b0 + 1][0:P, 0:WW],
                                               v16(VB + c * 2 * KW + 2 * (jlo + 1), 0, P, WW), ALU.mult)
                    s.emit("vector", fn_w2, inc="v")

                    def fn_w3(e, WW=WW):
                        return e.tensor_tensor(v32(SCc, 0, P, WW), v32(SCa, 0, P, WW),
                                               v32(SCb, 0, P, WW), ALU.add)
                    s.emit("vector", fn_w3, inc="v")

                    def fn_w4(e, c=c, b0=b0, jlo=jlo, WW=WW):
                        return e.tensor_tensor(v32(SCa, 0, P, WW), pb[b0 + 2][0:P, 0:WW],
                                               v16(VB + c * 2 * KW + 2 * (jlo + 2), 0, P, WW), ALU.mult)
                    v_w4 = s.emit("vector", fn_w4, inc="v")
                    trio_v[b0] = v_w4
                    wfe_trio_guard[b0] = v_w4

                    def fn_w5(e, c=c, jlo=jlo, WW=WW):
                        return e.tensor_tensor(v16(VB + c * 2 * KW + 2 * (jlo - 1), 0, P, WW),
                                               v32(SCc, 0, P, WW),
                                               v32(SCa, 0, P, WW), ALU.add)
                    vw = s.emit("vector", fn_w5, inc="v")

                wn_guard[win_pp] = t_wf
                win_done_at.append(vw)
                return vw

            pending_window.append(emit_window)
            win_pp ^= 1

        while pending_window:
            v_win_done = pending_window.pop(0)(v_win_done, v_subs_prev)

        wb_gu = {}
        wb_wd = {}
        for sb in range(6):
            for kind, srcw, scr in (("g", wg, gscr), ("u", wu, uscr)):
                slot = WVS + (0 if kind == "g" else 8_192)
                w = [("t", t_v_last)]
                if sb >= 1:
                    w.append(("q", wb_gu[(sb - 1, kind)]))
                cw = wdma(
                    lambda slot=slot: v16(slot, 0, P, 4096).rearrange(
                        "p (h m) -> p h m", h=4),
                    lambda srcw=srcw, sb=sb: srcw[4 * sb:4 * sb + 4, :, :].rearrange(
                        "h p m -> p h m"),
                    waits=w)
                wb_gu[(sb, kind)] = sdma(
                    lambda scr=scr, sb=sb: scr[:, 4096 * sb:4096 * sb + 4096],
                    lambda slot=slot: v16(slot, 0, P, 4096),
                    waits=[("w", cw)])
        w_wo = load_attn_weight("wo", WB1, pe_free=max(t_red_last, t_k_last))
        for pi, oc0 in enumerate((2, 4, 6, 0)):
            slot = KB + (pi % 2) * 12_288
            w = [("v", qk_last[0])]
            if pi >= 2:
                w.append(("q", wb_wd[pi - 2]))
            cw = wdma(
                lambda slot=slot: v16(slot, 0, P, 6144).rearrange(
                    "p (o m) -> p o m", o=2),
                lambda oc0=oc0: wdt[oc0:oc0 + 2, :, :].rearrange("o p m -> p o m"),
                waits=w)
            wb_wd[pi] = sdma(
                lambda oc0=oc0: dscr[:, 3072 * oc0:3072 * oc0 + 6144],
                lambda slot=slot: v16(slot, 0, P, 6144),
                waits=[("w", cw)])

        def rhs_vout(ic, t0, W):
            return v16(VB + ic * 2 * KW + 2 * t0, 0, P, W)

        def dst_ao(e, oc, t0, W, ps):
            return e.activation(v32(AO + oc * 4 * 2048 + 4 * t0, 0, P, W), ps[0:P, 0:W],
                                AF.Identity, bias=bias_ap(4, oc))
        t_wo, a_wo_evs = emit_proj(WB1, 4, rhs_vout, dst_ao, G4,
                                   lambda ti: [("v", win_done_at[4])],
                                   mm_extra=[("w", w_wo)],
                                   ev_extra=[("q", wb_gu[(5, "u")])])
        gd["bankev_0"] = a_wo_evs[-2]
        gd["bankev_1"] = a_wo_evs[-1]
        ab_v = []
        for ti2, (t0w, Ww) in enumerate(G4):
            vlast_ab = 0
            for oc in range(CH):
                def fn_ab(e, oc=oc, t0w=t0w, Ww=Ww):
                    return e.tensor_copy(v16(AB + oc * 2 * 2048 + 2 * t0w, 0, P, Ww),
                                         v32(AO + oc * 4 * 2048 + 4 * t0w, 0, P, Ww))
                vlast_ab = s.emit("vector", fn_ab,
                                  waits=[("a", a_wo_evs[ti2 * CH + oc])], inc="v")
            ab_v.append(vlast_ab)

        w_x0a = wdma(lambda: v32(X0R + 3 * 4 * 2048, 0, P, 5 * 2048).rearrange(
                         "p (c m) -> p c m", c=5),
                     lambda: x_in[:, 3:8, 1:2049], waits=[("t", t_wo)])
        d_x0r_b = dma(lambda: v32(X0R, 0, P, 3 * 2048).rearrange("p (c m) -> p c m", c=3),
                      lambda: x_in[:, 0:3, 1:2049],
                      waits=[("t", t_wo), ("q", wb_wd[3])])

        lnb_state = {"v": 0, "g": 0, "v67": 0, "t3": None, "tiles": []}
        NPOOL = 0

        lnb_pend = [None]

        def lnb_cb(td):
            t0, W = td["t0"], td["W"]
            if t0 != 1536 and lnb_pend[0] is not None:
                fn, lnb_pend[0] = lnb_pend[0], None
                fn()
            a_evR, a_evB = emit_rb_evac(td, None, as_f32=True)
            rs_, bs_ = td["rslot"], td["bslot"]
            if t0 == 1536:
                lnb_state["t3"] = (td, t0, W, rs_, bs_, a_evR, a_evB)
            else:
                lnb_pend[0] = lambda: lnb_resid(td, t0, W, rs_, bs_, a_evR, a_evB)

        def lnb_resid(td, t0, W, rs_, bs_, a_evR, a_evB,
                      chunks=(6, 7, 0, 1, 2, 3, 4, 5), final=True):
            v_last = 0
            g_last = 0
            for c in chunks:
                pool = c >= CH - NPOOL
                eng, sem, sc = ("gpsimd", "g", T12_P) if pool else ("vector", "v", T12_B)

                def fn_r1(e, c=c, t0=t0, W=W, rs_=rs_, sc=sc):
                    return e.tensor_tensor(v32(sc, 0, P, W),
                                           v32(AO + c * 4 * 2048 + 4 * t0, 0, P, W),
                                           v32(rs_, 0, P, W), ALU.mult)
                wx = ("d", d_x0r_b) if c < 3 else ("w", w_x0a)
                s.emit(eng, fn_r1,
                       waits=[("a", a_evR), ("a", a_evB), wx], inc=sem)

                def fn_r2(e, W=W, bs_=bs_, sc=sc):
                    return e.tensor_tensor(v32(sc + 2_048, 0, P, W), v32(sc, 0, P, W),
                                           v32(bs_, 0, P, W), ALU.add)
                s.emit(eng, fn_r2, inc=sem)

                def fn_r3(e, c=c, t0=t0, W=W, sc=sc):
                    return e.tensor_tensor(v32(AO + c * 4 * 2048 + 4 * t0, 0, P, W),
                                           v32(sc + 2_048, 0, P, W),
                                           v32(X0R + c * 4 * 2048 + 4 * t0, 0, P, W), ALU.add)
                last = s.emit(eng, fn_r3, inc=sem)
                if pool:
                    g_last = last
                else:
                    v_last = last
                if c == 7 and not pool:
                    lnb_state["v67"] = last
            if not final:
                return
            gd[f"rbt_v{td['par']}"] = v_last
            gd[f"rbt_g{td['par']}"] = g_last
            lnb_state["v"] = v_last
            lnb_state["g"] = g_last
            lnb_state["tiles"].append(v_last)

        emit_ln(AB, 2048, G4, lambda tix: ("v", ab_v[tix]),
                XSQ_B, (2, 3, 4, 5), lnb_cb)
        t3 = lnb_state["t3"]
        lnb_resid(*t3, chunks=(6, 7), final=False)
        fn, lnb_pend[0] = lnb_pend[0], None
        fn()
        lnb_resid(*t3, chunks=(0, 1, 2, 3, 4, 5), final=True)
        v_x1 = lnb_state["v"]
        g_x1 = lnb_state["g"]
        v_x1_tiles = lnb_state["tiles"]

        v_x2 = 0
        g_x2 = 0
        t_down_prev = 0
        o_prev_dma = 0
        nout = [0]
        silu_guard = {}
        h_guard = {}
        gu_mm_guard = {}
        lnd_pend = [None]
        pre_cast = [0]

        def emit_lnc_casts(half_, hbase_, tdp):
            last = 0
            for c in range(CH):
                v_need = (max(v_x1_tiles[1], lnb_state["v67"]) if half_ == 0
                          else v_x1)
                w = [("v", v_need), ("g", g_x1), ("t", tdp),
                     ("t", gd["xsq_t"]), ("q", wb_gu[(5, "u")])]

                def fn(e, c=c, hbase_=hbase_):
                    return e.activation(v16(LNC + c * 2 * 1024, 0, P, 1024),
                                        v32(AO + c * 4 * 2048 + 4 * hbase_, 0, P, 1024),
                                        AF.Copy)
                last = s.emit("scalar", fn, waits=w, inc="a")
            return last
        pending_od = [None]

        def flush_od(eng="sync"):
            if pending_od[0] is not None:
                ods, pending_od[0] = pending_od[0], None
                for fn, w in ods:
                    s.emit(eng, fn, waits=w, inc="o", inc_by=16)
        prev_lnd_evB = 0
        for half in range(2):
            flush_od()
            hbase = half * 1024
            if half == 0:
                a_lnc_cast = emit_lnc_casts(0, 0, t_down_prev)
            else:
                a_lnc_cast = pre_cast[0]
            gridh = [(0, 512), (512, 512)]
            lnc_state = {"v": 0, "tiles": []}

            lnc_pend = [None]

            def lnc_cb(td):
                if lnd_pend[0] is not None:
                    fn, lnd_pend[0] = lnd_pend[0], None
                    fn()
                if lnc_pend[0] is not None:
                    fn, lnc_pend[0] = lnc_pend[0], None
                    fn()
                t0, W = td["t0"], td["W"]
                a_evR, a_evB = emit_rb_evac(td, None, as_f32=False)
                rs_, bs_ = td["rslot"], td["bslot"]
                lnc_pend[0] = lambda: lnc_norm(td, t0, W, rs_, bs_, a_evR, a_evB)

            def lnc_norm(td, t0, W, rs_, bs_, a_evR, a_evB):
                v_last = 0
                for c in range(CH):
                    def fn_n1(e, c=c, t0=t0, W=W, rs_=rs_):
                        return e.tensor_tensor(v16(NSCF, 0, P, W),
                                               v16(LNC + c * 2 * 1024 + 2 * t0, 0, P, W),
                                               v16(rs_, 0, P, W), ALU.mult)
                    s.emit("vector", fn_n1,
                           waits=[("a", a_evR), ("a", a_evB), ("g", s.val("g"))], inc="v")

                    def fn_n2(e, c=c, t0=t0, W=W, bs_=bs_):
                        return e.tensor_tensor(v16(LNC + c * 2 * 1024 + 2 * t0, 0, P, W),
                                               v16(NSCF, 0, P, W),
                                               v16(bs_, 0, P, W), ALU.add)
                    v_last = s.emit("vector", fn_n2, inc="v")
                gd[f"rbt_v{td['par']}"] = v_last
                lnc_state["v"] = v_last
                lnc_state["tiles"].append(v_last)

            emit_ln(LNC, 1024, gridh, lambda tix: ("a", a_lnc_cast), XSQ_F, (0, 1, 4, 5), lnc_cb)
            fn, lnc_pend[0] = lnc_pend[0], None
            fn()
            flush_od()
            v_lnc = lnc_state["v"]
            v_lnc_tiles = lnc_state["tiles"]

            v_h = 0
            w_slab = {}
            for sb in range(6):
                for kind, srcw, scr, bfo in (("g", wg, gscr, GBF), ("u", wu, uscr, UBF)):
                    w = []
                    if sb >= 2 and (sb - 2, kind) in gu_mm_guard:
                        w.append(("t", gu_mm_guard[(sb - 2, kind)]))
                    if sb == 0:
                        w += [("t", s.val("t")), ("a", s.val("a"))]
                    elif sb == 1:
                        w += [("v", s.val("v")), ("a", s.val("a")), ("g", s.val("g")),
                              ("o", o_prev_dma), ("t", s.val("t"))]
                    bslot = bfo + (sb % 2) * 8_192
                    w.append(("q", wb_gu[(sb, kind)]))
                    if kind == "g":
                        cnt = sdma(
                            lambda bslot=bslot: v16(bslot, 0, P, 4096),
                            lambda scr=scr, sb=sb: scr[:, 4096 * sb:4096 * sb + 4096],
                            waits=w)
                        w_slab[(sb, kind)] = ("q", cnt)
                    else:
                        cnt = wdma(
                            lambda bslot=bslot: v16(bslot, 0, P, 4096),
                            lambda scr=scr, sb=sb: scr[:, 4096 * sb:4096 * sb + 4096],
                            waits=w)
                        w_slab[(sb, kind)] = ("w", cnt)
                for hq in range(4):
                    hc = 4 * sb + hq
                    for tt in range(2):
                        t0 = tt * 512
                        gbank, ubank = tt, 2 + tt

                        def fn_gmm(e, sb=sb, hq=hq, t0=t0, gbank=gbank):
                            ins = None
                            for ic in range(CH):
                                ins = e.matmul(pb[gbank][0:P, 0:512],
                                               v16(GBF + (sb % 2) * 8_192 + hq * 2_048
                                                   + 2 * ic * 128, 0, P, 128),
                                               v16(LNC + ic * 2 * 1024 + 2 * t0, 0, P, 512),
                                               start=(ic == 0), stop=(ic == CH - 1))
                            return ins
                        wg_ = [w_slab[(sb, "g")], ("v", v_lnc_tiles[tt])]
                        if (hc - 2, tt) in silu_guard:
                            wg_.append(("a", silu_guard[(hc - 2, tt)]))
                        elif hc <= 1:
                            wg_.append(("a", s.val("a")))
                        t_g = s.emit("tensor", fn_gmm, waits=wg_, inc="t")
                        if hq == 3 and tt == 1:
                            gu_mm_guard[(sb, "g")] = t_g

                        def fn_umm(e, sb=sb, hq=hq, t0=t0, ubank=ubank):
                            ins = None
                            for ic in range(CH):
                                ins = e.matmul(pb[ubank][0:P, 0:512],
                                               v16(UBF + (sb % 2) * 8_192 + hq * 2_048
                                                   + 2 * ic * 128, 0, P, 128),
                                               v16(LNC + ic * 2 * 1024 + 2 * t0, 0, P, 512),
                                               start=(ic == 0), stop=(ic == CH - 1))
                            return ins
                        wu_ = [w_slab[(sb, "u")]]
                        if (hc - 2, tt) in h_guard:
                            wu_.append(("v", h_guard[(hc - 2, tt)]))
                        elif hc <= 1:
                            wu_.append(("v", s.val("v")))
                        t_u = s.emit("tensor", fn_umm, waits=wu_, inc="t")
                        if hq == 3 and tt == 1:
                            gu_mm_guard[(sb, "u")] = t_u

                        sg_slot = SGT + tt * 1_024

                        def fn_silu(e, gbank=gbank, sg_slot=sg_slot):
                            return e.activation(v16(sg_slot, 0, P, 512), pb[gbank][0:P, 0:512],
                                                AF.Silu, scale=DEQ)
                        ws_ = [("t", t_g)]
                        if hc == 0:
                            ws_.append(("g", s.val("g")))
                        a_silu = s.emit("scalar", fn_silu, waits=ws_, inc="a")
                        silu_guard[(hc, tt)] = a_silu

                        def fn_h(e, hc=hc, t0=t0, ubank=ubank, sg_slot=sg_slot):
                            return e.tensor_tensor(v16(HBUF + hc * 2_048 + 2 * t0, 0, P, 512),
                                                   v16(sg_slot, 0, P, 512),
                                                   pb[ubank][0:P, 0:512], ALU.mult)
                        v_h = s.emit("vector", fn_h, waits=[("a", a_silu), ("t", t_u)], inc="v")
                        h_guard[(hc, tt)] = v_h

            wd_slot_guard = [0, 0, 0, 0]
            for tt in range(2):
                t0 = tt * 512
                t_wd_mm = {}
                w_wd = {}
                a_down_last = 0
                oc_order = [2, 3, 4, 5, 6, 7, 0, 1]
                def emit_wd_oc(oci, extra_t):
                    oc = oc_order[oci]
                    slot = WDBF + (oci % 4) * 6_144
                    w = [("v", v_h), ("t", extra_t), ("q", wb_wd[oci // 2])]
                    if oci % 2 == 0:
                        cnt = wdma(lambda slot=slot: v16(slot, 0, P, 3072),
                                   lambda oc=oc: dscr[:, 3072 * oc:3072 * oc + 3072],
                                   waits=w)
                        w_wd[oci] = [("w", cnt)]
                    else:
                        cnt = sdma(lambda slot=slot: v16(slot, 0, P, 3072),
                                   lambda oc=oc: dscr[:, 3072 * oc:3072 * oc + 3072],
                                   waits=w)
                        w_wd[oci] = [("q", cnt)]

                for _oci in range(4):
                    emit_wd_oc(_oci, wd_slot_guard[_oci % 4] or s.val("t"))
                flush_od("scalar")
                for oci, oc in enumerate(oc_order):
                    wdslot = WDBF + (oci % 4) * 6_144

                    def fn_dmm(e, t0=t0, oc=oc, wdslot=wdslot):
                        ins = None
                        for hcc in range(HC):
                            ins = e.matmul(pb[oc][0:P, 0:512],
                                           v16(wdslot + 2 * hcc * 128, 0, P, 128),
                                           v16(HBUF + hcc * 2_048 + 2 * t0, 0, P, 512),
                                           start=(hcc == 0), stop=(hcc == HC - 1))
                        return ins
                    wm = [*w_wd[oci], ("v", v_h)]
                    if oc in (0, 1):
                        wm.append(("a", prev_lnd_evB))
                        wm.append(("a", s.val("a")))
                    t_dmm = s.emit("tensor", fn_dmm, waits=wm, inc="t")
                    t_wd_mm[oci] = t_dmm
                    wd_slot_guard[oci % 4] = t_dmm
                    if oci + 4 <= 7:
                        emit_wd_oc(oci + 4, t_dmm)

                    def fn_dev(e, oc=oc):
                        e.activation(v32(FFO + oc * 4 * 512, 0, P, 512),
                                     pb[oc][0:P, 0:512], AF.Identity, scale=DEQ * DEQ)
                        e.activation(v16(FBT + oc * 2 * 512, 0, P, 512),
                                     pb[oc][0:P, 0:512], AF.Identity, scale=DEQ * DEQ)
                        return e.activation(v16(XSQ_F + oc * 2 * 512, 0, P, 512),
                                            pb[oc][0:P, 0:512], AF.Square, scale=DEQ * DEQ)
                    o_need = max(0, o_prev_dma - 16 * (3 - oc // 2))
                    a_down_last = s.emit("scalar", fn_dev,
                                         waits=[("t", t_dmm), ("o", o_need)], inc="a")

                if half == 0 and tt == 1:
                    pre_cast[0] = emit_lnc_casts(1, 1024, t_wd_mm[7])

                lnd_state = {"v": 0, "evB": 0}

                def lnd_cb(td):
                    a_evR, a_evB = emit_rb_evac(td, None, as_f32=True)
                    lnd_state["evR"] = a_evR
                    lnd_state["evB"] = a_evB

                lnd_tiles = emit_ln(FBT, 512, [(0, 512)], lambda tix: ("a", a_down_last),
                                    XSQ_F, (0, 1), lnd_cb, skip_square=True)
                td = lnd_tiles[0]
                a_evR, a_evB = lnd_state["evR"], lnd_state["evB"]
                prev_lnd_evB = a_evB

                def lnd_resid(t0=t0, hbase=hbase, a_evR=a_evR, a_evB=a_evB):
                    v_x2 = 0
                    g_x2 = 0
                    pair_done = [0, 0, 0, 0]
                    for c in range(CH):
                        pool = c >= CH - NPOOL
                        eng, sem = ("gpsimd", "g") if pool else ("vector", "v")
                        sc1, sc2 = (NSCF, SGT) if pool else (T12_F, T12_F + 2_048)

                        def fn_r1(e, c=c, sc1=sc1):
                            return e.tensor_tensor(v32(sc1, 0, P, 512),
                                                   v32(FFO + c * 4 * 512, 0, P, 512),
                                                   v32(RBT, 0, P, 512), ALU.mult)
                        s.emit(eng, fn_r1, waits=[("a", a_evR), ("a", a_evB)], inc=sem)

                        def fn_r2(e, sc1=sc1, sc2=sc2):
                            return e.tensor_tensor(v32(sc2, 0, P, 512), v32(sc1, 0, P, 512),
                                                   v32(RBT + 2_080, 0, P, 512), ALU.add)
                        s.emit(eng, fn_r2, inc=sem)

                        def fn_r3(e, c=c, t0=t0, hbase=hbase, sc2=sc2):
                            return e.tensor_tensor(v32(FFO + c * 4 * 512, 0, P, 512),
                                                   v32(sc2, 0, P, 512),
                                                   v32(AO + c * 4 * 2048 + 4 * (hbase + t0), 0, P, 512),
                                                   ALU.add)
                        if pool:
                            g_x2 = s.emit(eng, fn_r3, inc=sem)
                        else:
                            v_x2 = s.emit(eng, fn_r3, inc=sem)
                        if c % 2 == 1 and not pool:
                            pair_done[c // 2] = v_x2
                    gd["rbt_v0"] = v_x2
                    gd["rbt_g0"] = g_x2

                    ods = []
                    for k in range(4):
                        def fn_od(e, t0=t0, hbase=hbase, k=k):
                            return e.dma_start(
                                out=out[:, 2 * k:2 * k + 2, hbase + t0: hbase + t0 + 512],
                                in_=v32(FFO + k * 2 * 4 * 512, 0, P, 2 * 512).rearrange(
                                    "p (c m) -> p c m", c=2))
                        w = [("v", pair_done[k])]
                        if k == 3:
                            w.append(("g", g_x2))
                        ods.append((fn_od, w))
                    pending_od[0] = ods

                nout[0] += 4
                o_prev_dma = 16 * nout[0]
                if tt == 0:
                    lnd_resid()
                else:
                    lnd_pend[0] = lnd_resid
                t_down_prev = t_wd_mm[7]

        if lnd_pend[0] is not None:
            fn, lnd_pend[0] = lnd_pend[0], None
            fn()
        flush_od()
        if _kdump is not None:
            s.trunc = None
            doff = int(_kdump)
            allw = [(k, s.val(k)) for k in ("t", "a", "v", "g")] + \
                   [("d", s.val("d")), ("w", s.val("w")), ("o", s.val("o"))]

            def fn_dump(e, doff=doff):
                return e.dma_start(out=out[:],
                                   in_=v32(doff, 0, P, CH * 2048).rearrange(
                                       "p (c m) -> p c m", c=CH))
            s.emit("sync", fn_dump, waits=allw, inc="o", inc_by=16)
            nout[0] += 1
        s.emit("sync", lambda e: e.nop(), waits=[("o", 16 * nout[0])])

        @block.sync
        def _(sync):
            s.run("sync", sync, sems)

        @block.tensor
        def _(tensor):
            s.run("tensor", tensor, sems)

        @block.scalar
        def _(scalar):
            s.run("scalar", scalar, sems)

        @block.vector
        def _(vector):
            s.run("vector", vector, sems)

        @block.gpsimd
        def _(gpsimd):
            s.run("gpsimd", gpsimd, sems)

    return nc


_BF = ml_dtypes.bfloat16


def _tile_weight(w):
    wt = np.ascontiguousarray(w.T)
    return np.ascontiguousarray(wt.reshape(CH, P, w.shape[0]).transpose(1, 0, 2))


def _tile_gu(wint):
    wt = np.ascontiguousarray(wint.T)
    a = wt.reshape(CH, P, HC, P).transpose(2, 1, 0, 3)
    return np.ascontiguousarray(a.reshape(HC, P, CH * P))


def _tile_wd(wint):
    wt = np.ascontiguousarray(wint.T)
    a = wt.reshape(HC, P, CH, P).transpose(2, 1, 0, 3)
    return np.ascontiguousarray(a.reshape(CH, P, HC * P))


def kernel(L0_attn_x, L0_attn_Wp, L0_attn_bp, L0_attn_Wq, L0_attn_bq, L0_attn_Wk,
           L0_attn_bk, L0_attn_Wv, L0_attn_bv, L0_attn_Wo, L0_attn_bo,
           L0_ffn_w_gate, L0_ffn_w_up, L0_ffn_w_down):
    x = np.asarray(L0_attn_x, dtype=np.float32)
    S = x.shape[1]
    xt = np.ascontiguousarray(x[0].T)

    wp = _tile_weight(np.asarray(L0_attn_Wp, np.float32))
    wq = _tile_weight(np.asarray(L0_attn_Wq, np.float32))
    wk = _tile_weight(np.asarray(L0_attn_Wk, np.float32))
    wv = _tile_weight(np.asarray(L0_attn_Wv, np.float32))
    wo = _tile_weight(np.asarray(L0_attn_Wo, np.float32))
    wg_t = _tile_gu(np.asarray(L0_ffn_w_gate, np.int32))
    wu_t = _tile_gu(np.asarray(L0_ffn_w_up, np.int32))
    wd_t = _tile_wd(np.asarray(L0_ffn_w_down, np.int32))

    bias_all = np.zeros((P, CH * 5), np.float32)
    for wi_, b in enumerate((L0_attn_bp, L0_attn_bq, L0_attn_bk, L0_attn_bv, L0_attn_bo)):
        b = np.asarray(b, np.float32).reshape(CH, P)
        for oc in range(CH):
            bias_all[:, oc * 5 + wi_] = b[oc]

    e16 = np.zeros((P, CH * H), np.float32)
    for c in range(CH):
        for pp_ in range(P):
            e16[pp_, c * H + 2 * c + (pp_ // 64)] = 1.0
    eb = np.zeros((H, CH * P), np.float32)
    for c in range(CH):
        for pp_ in range(P):
            eb[2 * c + (pp_ // 64), c * P + pp_] = 1.0

    nc = build_program()
    in_maps = []
    for core in range(NCORE):
        start = core * T
        idx = (np.arange(start - 1, start + T + 1) % S)
        xc = np.ascontiguousarray(xt[:, idx])
        x_tiled = np.ascontiguousarray(xc.reshape(CH, P, THALO).transpose(1, 0, 2))
        in_maps.append(dict(
            x_in=x_tiled, wp=wp, wk=wk, wv=wv, wq=wq, wo=wo,
            biases=bias_all, e16=e16.astype(_BF), ebc=eb.astype(_BF),
            wg=wg_t, wu=wu_t, wdt=wd_t,
        ))
    res = run_bass_kernel_spmd(nc, in_maps, core_ids=list(range(NCORE)))
    shards = []
    for core in range(NCORE):
        o = np.asarray(res.results[core]["out"])
        shards.append(o.transpose(1, 0, 2).reshape(D, T))
    y = np.concatenate(shards, axis=1)
    return np.ascontiguousarray(y.T)[None, :, :]


# version 67
# speedup vs baseline: 1.0156x; 1.0130x over previous
import numpy as np
import ml_dtypes
from contextlib import ExitStack

import concourse.bass as bass
import concourse.mybir as mybir
from concourse.bass_utils import run_bass_kernel_spmd

F32 = mybir.dt.float32
BF16 = mybir.dt.bfloat16
I32 = mybir.dt.int32
U8 = mybir.dt.uint8
AF = mybir.ActivationFunctionType
ALU = mybir.AluOpType

P = 128
CH = 8
D = 1024
T = 2048
THALO = 2050
H = 16
NCORE = 8
HID = 3072
HC = 24
SCALE_Q = 0.125
DEQ = 1.0 / 127.0
EPS = 1e-5

G5 = [(i * 410, 410) for i in range(5)]
G4 = [(i * 512, 512) for i in range(4)]

AR_BYTES = 212_832
RA = 0
RB = 65_664
RW = 131_456
RS = 180_608

XB = RA
XP = RA + 32_832
AO = RA

X0H = RB
KB = RB
VB = RB + 32_896
WOSTG = RB
X0R = RB
HBUF = RB
XSQ_F = RB + 49_152
T12_F = RB + 57_344
NSCF = RB + 61_440
SGT = RB + 63_520

STG = RW
WB0 = RW + 16_384
WB1 = RW + 32_768
WVS = STG
SIG = XB
SC123 = XB + 6 * 1_664
QSB = XB + 14_976
TMP = XB + 21_632
WFE = XB + 28_288
AB = RW
XSQ_B = RW + 32_768
T12_B = RW + 40_960
LNC = RW
GBF = RW + 16_384
UBF = RW + 32_768
WDBF = RW
FFO = RW + 24_576
FBT = RW + 40_960

ST_SL = RS
RBMR = ST_SL + 6 * 2_080
RBT = RBMR + 2 * 1_040
WN = RBT + 4 * 2_080
CONST_E16 = WN + 6 * 832
CONST_EB = CONST_E16 + 256
CONST_BIAS = CONST_EB + 2_048
NSCA = CONST_BIAS + 160
ONESC = NSCA + 832
ROWS1 = ONESC + 32
EPSC = ROWS1 + 512
RS_END = EPSC + 32
assert RS_END <= AR_BYTES, RS_END

T12_P = RW + 45_056

KW = 2056


class Sched:
    ENG = ("sync", "tensor", "scalar", "vector", "gpsimd")

    def __init__(self):
        self.q = {e: [] for e in self.ENG}
        self.cnt = {}
        self.last_wait = {}
        self.nemit = 0
        self.trunc = None

    def val(self, sem):
        return self.cnt.get(sem, 0)

    def emit(self, eng, fn, waits=(), inc=None, inc_by=1):
        self.nemit += 1
        if self.trunc is not None and self.nemit > self.trunc:
            return self.cnt.get(inc, 0) + inc_by if inc is not None else None
        newv = None
        if inc is not None:
            self.cnt[inc] = self.cnt.get(inc, 0) + inc_by
            newv = self.cnt[inc]
        w = []
        for sn, sv in waits:
            if not sv:
                continue
            key = (eng, sn)
            if self.last_wait.get(key, -1) >= sv:
                continue
            self.last_wait[key] = sv
            w.append((sn, sv))
        self.q[eng].append((fn, tuple(w), inc, inc_by))
        return newv

    def run(self, eng_name, engine, sems):
        for fn, waits, inc, inc_by in self.q[eng_name]:
            for sn, sv in waits:
                engine.wait_ge(sems[sn], sv)
            ins = fn(engine)
            if inc is not None:
                ins.then_inc(sems[inc], inc_by)


def build_program():
    import os
    _ktrunc = os.environ.get("KTRUNC")
    _kdump = os.environ.get("KDUMP")
    nc = bass.Bass("TRN2", target_bir_lowering=False, debug=False)

    x_in = nc.dram_tensor("x_in", [P, CH, THALO], F32, kind="ExternalInput")
    wts = {w: nc.dram_tensor(w, [P, CH, D], F32, kind="ExternalInput")
           for w in ("wp", "wk", "wv", "wq", "wo")}
    biases = nc.dram_tensor("biases", [P, CH * 5], F32, kind="ExternalInput")
    e16 = nc.dram_tensor("e16", [P, CH * H], BF16, kind="ExternalInput")
    ebc = nc.dram_tensor("ebc", [H, CH * P], BF16, kind="ExternalInput")
    wg = nc.dram_tensor("wg", [HC, P, CH * P], I32, kind="ExternalInput")
    wu = nc.dram_tensor("wu", [HC, P, CH * P], I32, kind="ExternalInput")
    wdt = nc.dram_tensor("wdt", [CH, P, HC * P], I32, kind="ExternalInput")
    out = nc.dram_tensor("out", [P, CH, T], F32, kind="ExternalOutput")
    gscr = nc.dram_tensor("gscr", [P, HC * 1024], BF16, kind="Internal")
    uscr = nc.dram_tensor("uscr", [P, HC * 1024], BF16, kind="Internal")
    dscr = nc.dram_tensor("dscr", [P, CH * 3072], BF16, kind="Internal")

    s = Sched()
    if _ktrunc:
        s.trunc = int(_ktrunc)

    with ExitStack() as st:
        en = st.enter_context
        arena = en(nc.sbuf_tensor("arena", [P, AR_BYTES], U8))
        a16 = arena.bitcast(BF16)
        a32 = arena.bitcast(F32)
        ai32 = arena.bitcast(I32)
        pb = [en(nc.psum_tensor(f"pb{i}", [P, 512], F32)) for i in range(8)]
        sems = {n: en(nc.semaphore(n)) for n in ("d", "t", "a", "v", "g", "o", "w", "q")}
        block = en(nc.Block())

        def v16(off, p0, np_, n):
            assert off % 2 == 0 and off + 2 * n <= AR_BYTES
            return a16[p0:p0 + np_, off // 2: off // 2 + n]

        def v32(off, p0, np_, n):
            assert off % 4 == 0 and off + 4 * n <= AR_BYTES
            return a32[p0:p0 + np_, off // 4: off // 4 + n]

        def vi32(off, p0, np_, n):
            assert off % 4 == 0 and off + 4 * n <= AR_BYTES
            return ai32[p0:p0 + np_, off // 4: off // 4 + n]

        ndma = [0]

        def dma(dst_fn, src_fn, waits=()):
            ndma[0] += 1

            def fn(e):
                return e.dma_start(out=dst_fn(), in_=src_fn())
            s.emit("sync", fn, waits=waits, inc="d", inc_by=16)
            return 16 * ndma[0]

        nw = [0]

        def wdma(dst_fn, src_fn, waits=()):
            nw[0] += 1

            def fn(e):
                return e.dma_start(out=dst_fn(), in_=src_fn())
            s.emit("gpsimd", fn, waits=waits, inc="w", inc_by=16)
            return 16 * nw[0]

        nq = [0]

        def sdma(dst_fn, src_fn, waits=()):
            nq[0] += 1

            def fn(e):
                return e.dma_start(out=dst_fn(), in_=src_fn())
            s.emit("sync", fn, waits=waits, inc="q", inc_by=16)
            return 16 * nq[0]

        w_xb1 = wdma(lambda: v16(XB, 0, P, CH * 2052).rearrange("p (c m) -> p c m", c=CH)[:, :, 0:412],
                     lambda: x_in[:, :, 0:412])
        w_xb = wdma(lambda: v16(XB, 0, P, CH * 2052).rearrange("p (c m) -> p c m", c=CH)[:, :, 412:THALO],
                    lambda: x_in[:, :, 412:THALO])

        d_e16 = dma(lambda: v16(CONST_E16, 0, P, CH * H), lambda: e16[:])
        d_eb = dma(lambda: v16(CONST_EB, 0, H, CH * P), lambda: ebc[:])
        d_bias = dma(lambda: v32(CONST_BIAS, 0, P, CH * 5), lambda: biases[:])

        def fn_mc(e):
            e.memset(v16(ONESC, 0, P, 1), 1.0)
            e.memset(v16(ROWS1, 0, 1, P), 1.0)
            e.memset(v16(ROWS1, 32, 1, P), 1.0)
            e.memset(v32(EPSC, 0, 1, 1), EPS)
            e.memset(v16(ROWS1 + 256, 32, 1, P), -1.0)
            return e.memset(v16(ROWS1 + 256, 0, 1, P), -1.0)
        v_consts = s.emit("vector", fn_mc, inc="v")

        def fn_warm(e):
            return e.activation(v16(NSCA, 0, 1, 1), v16(ONESC, 0, 1, 1), AF.Square)
        s.emit("scalar", fn_warm, waits=[("v", v_consts)], inc="a")

        def load_attn_weight(name, wbuf_off, pe_free=0):
            w = [("t", pe_free)] if pe_free else []
            return wdma(lambda: v16(wbuf_off, 0, P, 8192).rearrange("p (c m) -> p c m", c=CH),
                        lambda: wts[name][:], waits=w)

        w_wp = load_attn_weight("wp", WB0)
        w_wk = load_attn_weight("wk", WB1)

        gd = {"rbt_v0": 0, "rbt_v1": 0, "rbt_g0": 0, "rbt_g1": 0, "xsq_t": 0,
              "stats_v0": 0, "stats_v1": 0, "rb_t0": 0, "rb_t1": 0, "mr_t0": 0, "mr_t1": 0}
        for _b in range(8):
            gd[f"bankev_{_b}"] = 0
            gd[f"bankev_v{_b}"] = 0

        def emit_ln(src_off, src_stride_el, grid, wait_src_fn, xsq_off, sums_banks, tile_cb,
                    n_chunks=8, skip_square=False, sq_dve=False):
            tiles = []
            for tix, (t0, W) in enumerate(grid):
                par = tix % 2
                pbase = 32 * par
                if len(sums_banks) == 4:
                    bs, bq = sums_banks[2 * par], sums_banks[2 * par + 1]
                else:
                    bs, bq = sums_banks
                wait_src = wait_src_fn(tix)
                sq_last = None
                sq_sem = "v" if sq_dve else "a"
                if not skip_square:
                    for c in range(n_chunks):
                        if sq_dve:
                            def fn(e, c=c, t0=t0, W=W):
                                src_ap = v16(src_off + c * 2 * src_stride_el + 2 * t0, 0, P, W)
                                return e.tensor_tensor(v16(xsq_off + c * 2 * 512, 0, P, W),
                                                       src_ap, src_ap, ALU.mult)
                            sq_last = s.emit("vector", fn,
                                             waits=[wait_src, ("t", gd["xsq_t"])], inc="v")
                        else:
                            def fn(e, c=c, t0=t0, W=W):
                                return e.activation(v16(xsq_off + c * 2 * 512, 0, P, W),
                                                    v16(src_off + c * 2 * src_stride_el + 2 * t0,
                                                        0, P, W), AF.Square)
                            sq_last = s.emit("scalar", fn,
                                             waits=[wait_src, ("t", gd["xsq_t"])], inc="a")

                def fn_sum(e, t0=t0, W=W, bs=bs):
                    ins = None
                    for c in range(n_chunks):
                        ins = e.matmul(pb[bs][0:1, 0:W], v16(ONESC, 0, P, 1),
                                       v16(src_off + c * 2 * src_stride_el + 2 * t0, 0, P, W),
                                       start=(c == 0), stop=(c == n_chunks - 1))
                    return ins
                t_sum = s.emit("tensor", fn_sum,
                               waits=[wait_src, ("a", gd[f"bankev_{bs}"]),
                                      ("v", gd[f"bankev_v{bs}"]), ("v", v_consts)],
                               inc="t")

                def fn_ssq(e, W=W, bq=bq):
                    ins = None
                    for c in range(n_chunks):
                        ins = e.matmul(pb[bq][0:1, 0:W], v16(ONESC, 0, P, 1),
                                       v16(xsq_off + c * 2 * 512, 0, P, W),
                                       start=(c == 0), stop=(c == n_chunks - 1))
                    return ins
                t_ssq = s.emit("tensor", fn_ssq,
                               waits=[(sq_sem, sq_last if sq_last else 0), wait_src,
                                      ("a", gd[f"bankev_{bq}"]),
                                      ("v", gd[f"bankev_v{bq}"])], inc="t")
                gd["xsq_t"] = t_ssq

                m_of, nm_of, vv_of = (ST_SL + 2 * 2_080, ST_SL + 3 * 2_080, ST_SL + 4 * 2_080)
                inv_d = 1.0 / D

                def fn_m(e, W=W, bs=bs, pbase=pbase):
                    return e.tensor_scalar_mul(v32(m_of, pbase, 1, W), pb[bs][0:1, 0:W], inv_d)
                v_m = s.emit("vector", fn_m,
                             waits=[("t", t_sum), ("v", gd[f"stats_v{par}"])], inc="v")
                gd[f"bankev_v{bs}"] = v_m

                def fn_nm(e, W=W, pbase=pbase):
                    return e.scalar_tensor_tensor(v32(nm_of, pbase, 1, W), v32(m_of, pbase, 1, W),
                                                  -1.0, v32(m_of, pbase, 1, W),
                                                  ALU.mult, ALU.mult)
                s.emit("vector", fn_nm, inc="v")

                def fn_vv(e, W=W, bq=bq, pbase=pbase):
                    return e.scalar_tensor_tensor(v32(vv_of, pbase, 1, W),
                                                  pb[bq][0:1, 0:W], inv_d,
                                                  v32(nm_of, pbase, 1, W), ALU.mult, ALU.add)
                v_vv = s.emit("vector", fn_vv, waits=[("t", t_ssq)], inc="v")
                gd[f"bankev_v{bq}"] = v_vv

                def fn_ln_(e, W=W, pbase=pbase):
                    return e.activation(v32(vv_of, pbase, 1, W), v32(vv_of, pbase, 1, W), AF.Ln,
                                        bias=v32(EPSC, 0, 1, 1))
                s.emit("scalar", fn_ln_, waits=[("v", v_vv), ("v", v_consts)], inc="a")

                def fn_exp(e, W=W, pbase=pbase):
                    return e.activation(v16(RBMR, pbase, 1, W), v32(vv_of, pbase, 1, W),
                                        AF.Exp, scale=-0.5)
                a_rb = s.emit("scalar", fn_exp, waits=[("t", gd[f"rb_t{par}"])], inc="a")

                def fn_mr(e, W=W, pbase=pbase):
                    return e.tensor_tensor(v16(RBMR + 1_040, pbase, 1, W), v32(m_of, pbase, 1, W),
                                           v16(RBMR, pbase, 1, W), ALU.mult)
                v_mr = s.emit("vector", fn_mr, waits=[("a", a_rb), ("t", gd[f"mr_t{par}"])], inc="v")
                gd[f"stats_v{par}"] = v_mr

                def fn_bR(e, W=W, bs=bs, pbase=pbase):
                    return e.matmul(pb[bs][:, 0:W], v16(ROWS1, pbase, 1, P),
                                    v16(RBMR, pbase, 1, W), start=True, stop=True)
                t_bR = s.emit("tensor", fn_bR, waits=[("a", a_rb), ("v", v_m)], inc="t")
                gd[f"rb_t{par}"] = t_bR

                def fn_bB(e, W=W, bq=bq, pbase=pbase):
                    return e.matmul(pb[bq][:, 0:W], v16(ROWS1 + 256, pbase, 1, P),
                                    v16(RBMR + 1_040, pbase, 1, W), start=True, stop=True)
                t_bB = s.emit("tensor", fn_bB, waits=[("v", v_mr)], inc="t")
                gd[f"mr_t{par}"] = t_bB
                td = dict(t0=t0, W=W, t_bR=t_bR, t_bB=t_bB, par=par, bs=bs, bq=bq)
                tile_cb(td)
                tiles.append(td)
            return tiles

        def emit_rb_evac(td, banks, as_f32):
            del banks
            W = td["W"]
            par = td.get("par", 0)
            vv = v32 if as_f32 else v16
            rslot = RBT + par * 4_160
            bslot = RBT + par * 4_160 + 2_080

            def fn_evR(e, td=td, W=W, rslot=rslot):
                return e.activation(vv(rslot, 0, P, W), pb[td["bs"]][:, 0:W], AF.Copy)
            a_evR = s.emit("scalar", fn_evR,
                           waits=[("t", td["t_bR"]), ("v", gd[f"rbt_v{par}"]),
                                  ("g", gd[f"rbt_g{par}"])], inc="a")
            gd[f"bankev_{td['bs']}"] = a_evR

            def fn_evB(e, td=td, W=W, bslot=bslot):
                return e.activation(vv(bslot, 0, P, W), pb[td["bq"]][:, 0:W], AF.Copy)
            a_evB = s.emit("scalar", fn_evB, waits=[("t", td["t_bB"])], inc="a")
            gd[f"bankev_{td['bq']}"] = a_evB
            td["rslot"] = rslot
            td["bslot"] = bslot
            return a_evR, a_evB

        lna_state = {"v": 0}
        lna_pend = [None]

        def lna_cb(td):
            if lna_pend[0] is not None:
                fn, lna_pend[0] = lna_pend[0], None
                fn()
            t0, W = td["t0"], td["W"]
            a_evR, a_evB = emit_rb_evac(td, None, as_f32=False)
            rs_, bs_ = td["rslot"], td["bslot"]
            lna_pend[0] = lambda: lna_norm(td, t0, W, rs_, bs_, a_evR, a_evB)

        def lna_norm(td, t0, W, rs_, bs_, a_evR, a_evB):
            v_last = 0
            for c in range(CH):
                def fn_n1(e, c=c, t0=t0, W=W, rs_=rs_):
                    return e.tensor_tensor(v16(NSCA, 0, P, W),
                                           v16(XB + c * 2 * 2052 + 2 * t0, 0, P, W),
                                           v16(rs_, 0, P, W), ALU.mult)
                s.emit("vector", fn_n1, waits=[("a", a_evR), ("a", a_evB)], inc="v")

                def fn_n2(e, c=c, t0=t0, W=W, bs_=bs_):
                    return e.tensor_tensor(v16(XB + c * 2 * 2052 + 2 * t0, 0, P, W),
                                           v16(NSCA, 0, P, W),
                                           v16(bs_, 0, P, W), ALU.add)
                v_last = s.emit("vector", fn_n2, inc="v")
            td["v_norm"] = v_last
            gd[f"rbt_v{td['par']}"] = v_last
            lna_state["v"] = v_last

        lna_tiles = emit_ln(XB, 2052, G5,
                            lambda tix: ("w", w_xb1 if tix == 0 else w_xb),
                            XP, (6, 7, 4, 5), lna_cb)
        fn, lna_pend[0] = lna_pend[0], None
        fn()
        v_lna = lna_state["v"]

        def emit_proj_tile(state, wbuf_off, rhs_fn, dst_fn, ti, t0, W, tile_waits,
                           mm_extra=(), ev_extra=(), banks=(0, 1)):
            a_evs = state["a_evs"]
            for oc in range(CH):
                bank = banks[len(a_evs) % 2]
                waits = list(tile_waits) + list(mm_extra)
                if len(a_evs) >= 2:
                    waits.append(("a", a_evs[-2]))
                else:
                    waits.append(("a", s.val("a")))

                def fn_mm(e, oc=oc, t0=t0, W=W, bank=bank):
                    ins = None
                    for ic in range(CH):
                        ins = e.matmul(pb[bank][0:P, 0:W],
                                       v16(wbuf_off + 2 * (ic * 1024 + oc * 128), 0, P, 128),
                                       rhs_fn(ic, t0, W),
                                       start=(ic == 0), stop=(ic == CH - 1))
                    return ins
                state["t_last"] = s.emit("tensor", fn_mm, waits=waits, inc="t")

                def fn_ev(e, oc=oc, t0=t0, W=W, bank=bank):
                    return dst_fn(e, oc, t0, W, pb[bank])
                a_evs.append(s.emit("scalar", fn_ev,
                                    waits=[("t", state["t_last"])] + list(ev_extra), inc="a"))

        def emit_proj(wbuf_off, bias_idx, rhs_fn, dst_fn, grid, tile_wait_fn,
                      mm_extra=(), ev_extra=(), banks=(0, 1)):
            state = {"a_evs": [], "t_last": 0}
            for ti, (t0, W) in enumerate(grid):
                emit_proj_tile(state, wbuf_off, rhs_fn, dst_fn, ti, t0, W, tile_wait_fn(ti),
                               mm_extra=mm_extra, ev_extra=ev_extra, banks=banks)
            return state["t_last"], state["a_evs"]

        def bias_ap(w_idx, oc):
            return v32(CONST_BIAS + 4 * (oc * 5 + w_idx), 0, P, 1)

        def rhs_lna(ic, t0, W):
            return v16(XB + ic * 2 * 2052 + 2 * t0, 0, P, W)

        def rhs_xp(ic, t0, W):
            return v16(XP + ic * 2 * 2052 + 2 * t0, 0, P, W)

        def dst_xp(e, oc, t0, W, ps):
            return e.activation(v16(XP + oc * 2 * 2052 + 2 * t0, 0, P, W), ps[0:P, 0:W],
                                AF.Identity, bias=bias_ap(0, oc))
        t_xp, a_xp_evs = emit_proj(WB0, 0, rhs_lna, dst_xp, G5,
                                   lambda ti: [("v", lna_tiles[ti]["v_norm"])],
                                   mm_extra=[("w", w_wp), ("d", d_bias)])

        w_wv = load_attn_weight("wv", WVS)

        def fn_guards(e):
            e.memset(a16[:, KB // 2: KB // 2 + CH * KW: KW], 0.0)
            e.memset(a16[:, KB // 2 + 2051: KB // 2 + CH * KW: KW], 0.0)
            e.memset(a16[:, VB // 2: VB // 2 + CH * KW: KW], 0.0)
            return e.memset(a16[:, VB // 2 + 2051: VB // 2 + CH * KW: KW], 0.0)
        v_guards = s.emit("vector", fn_guards, inc="v")

        def dst_k(e, oc, t0, W, ps):
            return e.activation(v16(KB + oc * 2 * KW + 2 * (t0 + 1), 0, P, W), ps[0:P, 0:W],
                                AF.Identity, bias=bias_ap(2, oc))

        def dst_v(e, oc, t0, W, ps):
            return e.activation(v16(VB + oc * 2 * KW + 2 * (t0 + 1), 0, P, W), ps[0:P, 0:W],
                                AF.Identity, bias=bias_ap(3, oc))

        w_wq = load_attn_weight("wq", WB0, pe_free=t_xp)

        ps_state = {"a_evs": [], "t_last": 0}

        v_subs_prev = 0
        v_win_done = 0
        win_done_at = []
        wfe_trio_guard = {5: 0, 2: 0}
        wn_guard = [0, 0]
        t_red_last = 0
        a_q_evs = ps_state["a_evs"]
        win_pp = 0
        pending_window = []
        t_k_last = 0
        t_v_last = 0
        qk_last = [0]

        def super_tiles():
            nonlocal t_k_last, t_v_last
            for ti, (t0, W) in enumerate(G5):
                emit_proj_tile(ps_state, WB1, rhs_xp, dst_k, ti, t0, W,
                               [("a", a_xp_evs[ti * CH + CH - 1])],
                               mm_extra=[("w", w_wk)], ev_extra=[("v", v_guards)])
                t_k_last = ps_state["t_last"]
                emit_proj_tile(ps_state, WVS, rhs_xp, dst_v, ti, t0, W,
                               [("a", a_xp_evs[ti * CH + CH - 1])],
                               mm_extra=[("w", w_wv)], ev_extra=[("v", v_guards)])
                t_v_last = ps_state["t_last"]
                if ti >= 1:
                    yield ti - 1
            yield len(G5) - 1

        for ti in super_tiles():
            t0, W = G5[ti]
            q_evs = []
            for c in range(CH):
                bank = len(a_q_evs) % 2
                waits = [("w", w_wq), ("a", a_xp_evs[ti * CH + CH - 1])]
                if len(a_q_evs) >= 2:
                    waits.append(("a", a_q_evs[-2]))
                else:
                    waits.append(("a", s.val("a")))

                def fn_qmm(e, c=c, t0=t0, W=W, bank=bank):
                    ins = None
                    for ic in range(CH):
                        ins = e.matmul(pb[bank][0:P, 0:W],
                                       v16(WB0 + 2 * (ic * 1024 + c * 128), 0, P, 128),
                                       rhs_xp(ic, t0, W),
                                       start=(ic == 0), stop=(ic == CH - 1))
                    return ins
                t_q = s.emit("tensor", fn_qmm, waits=waits, inc="t")

                def fn_qev(e, c=c, W=W, bank=bank):
                    return e.activation(v16(QSB + c * 832, 0, P, W), pb[bank][0:P, 0:W],
                                        AF.Identity, bias=bias_ap(1, c))
                a_q = s.emit("scalar", fn_qev, waits=[("t", t_q)], inc="a")
                q_evs.append(a_q)
                a_q_evs.append(a_q)

            for n in range(3):
                v_tm = 0
                for c in range(CH):
                    def fn_tm(e, c=c, n=n, t0=t0, W=W):
                        return e.tensor_tensor(v16(TMP + c * 832, 0, P, W),
                                               v16(QSB + c * 832, 0, P, W),
                                               v16(KB + c * 2 * KW + 2 * (t0 + n), 0, P, W),
                                               ALU.mult)
                    v_tm = s.emit("vector", fn_tm,
                                  waits=[("a", q_evs[c]), ("t", t_red_last)], inc="v")
                qk_last[0] = v_tm

                def fn_red(e, n=n, W=W):
                    ins = None
                    for c in range(CH):
                        ins = e.matmul(pb[2 + n][0:H, 0:W],
                                       v16(CONST_E16 + c * 2 * H, 0, P, H),
                                       v16(TMP + c * 832, 0, P, W),
                                       start=(c == 0), stop=(c == CH - 1))
                    return ins
                t_red_last = s.emit("tensor", fn_red,
                                    waits=[("v", v_tm), ("v", v_subs_prev), ("d", d_e16),
                                           ("v", wfe_trio_guard[2])], inc="t")

            SG = [SIG + i * 1_664 for i in range(6)]

            def fn_s0(e, W=W):
                return e.activation(v32(SG[0], 0, H, W), pb[2][0:H, 0:W], AF.Copy)
            a_s0 = s.emit("scalar", fn_s0, waits=[("t", t_red_last)], inc="a")

            def fn_mx1(e, W=W):
                return e.tensor_tensor(v32(SG[1], 0, H, W), v32(SG[0], 0, H, W),
                                       pb[3][0:H, 0:W], ALU.max)
            s.emit("vector", fn_mx1, waits=[("a", a_s0), ("t", t_red_last)], inc="v")

            def fn_mx2(e, W=W):
                return e.tensor_tensor(v32(SG[2], 0, H, W), v32(SG[1], 0, H, W),
                                       pb[4][0:H, 0:W], ALU.max)
            s.emit("vector", fn_mx2, inc="v")
            for n in range(3):
                def fn_sub(e, n=n, W=W):
                    return e.tensor_tensor(v32(SG[3 + n], 0, H, W), pb[2 + n][0:H, 0:W],
                                           v32(SG[2], 0, H, W), ALU.subtract)
                v_subs_prev = s.emit("vector", fn_sub, inc="v")
            if pending_window:
                v_win_done = pending_window.pop(0)(v_win_done, v_subs_prev)

            def fn_exp(e, W=W):
                return e.activation(v32(SG[3], 0, H, 3 * 416), v32(SG[3], 0, H, 3 * 416),
                                    AF.Exp, scale=SCALE_Q)
            a_exp = s.emit("scalar", fn_exp, waits=[("v", v_subs_prev)], inc="a")

            def fn_ad1(e, W=W):
                return e.tensor_tensor(v32(SG[0], 0, H, W), v32(SG[3], 0, H, W),
                                       v32(SG[4], 0, H, W), ALU.add)
            s.emit("vector", fn_ad1, waits=[("a", a_exp)], inc="v")

            def fn_ad2(e, W=W):
                return e.tensor_tensor(v32(SG[1], 0, H, W), v32(SG[0], 0, H, W),
                                       v32(SG[5], 0, H, W), ALU.add)
            s.emit("vector", fn_ad2, inc="v")

            def fn_rec(e, W=W):
                return e.reciprocal(v32(SG[0], 0, H, W), v32(SG[1], 0, H, W))
            s.emit("vector", fn_rec, inc="v")

            wslots = [WN + (win_pp * 3 + n) * 832 for n in range(3)]
            v_wn = 0
            for n in range(3):
                def fn_wn(e, n=n, W=W, wslots=wslots):
                    return e.tensor_tensor(v16(wslots[n], 0, H, W), v32(SG[3 + n], 0, H, W),
                                           v32(SG[0], 0, H, W), ALU.mult)
                v_wn = s.emit("vector", fn_wn, waits=[("t", wn_guard[win_pp])], inc="v")

            def emit_window(vw_in, subs_now, t0=t0, W=W, wslots=wslots, v_wn=v_wn,
                            win_pp=win_pp):
                vw = vw_in
                jlo = t0
                jhi = t0 + W
                WW = jhi - jlo
                t_wf = 0
                trio_v = dict(wfe_trio_guard)
                for c in range(CH):
                    b0 = 5 if c % 2 == 0 else 2
                    for n in range(3):
                        def fn_wf(e, c=c, n=n, b0=b0, jlo=jlo, WW=WW, wslots=wslots, t0=t0):
                            return e.matmul(pb[b0 + n][0:P, 0:WW],
                                            v16(CONST_EB + c * 2 * P, 0, H, P),
                                            v16(wslots[n] + 2 * (jlo - t0), 0, H, WW),
                                            start=True, stop=True)
                        wfw = [("v", v_wn), ("v", trio_v[b0]), ("d", d_eb)]
                        if b0 == 2:
                            wfw.append(("v", subs_now))
                        t_wf = s.emit("tensor", fn_wf, waits=wfw, inc="t")
                    SCa, SCb, SCc = SC123, SC123 + 1_664, SC123 + 2 * 1_664

                    def fn_w1(e, c=c, b0=b0, jlo=jlo, WW=WW):
                        return e.tensor_tensor(v32(SCa, 0, P, WW), pb[b0][0:P, 0:WW],
                                               v16(VB + c * 2 * KW + 2 * jlo, 0, P, WW), ALU.mult)
                    s.emit("vector", fn_w1,
                           waits=[("t", t_wf), ("a", a_q_evs[-1]), ("v", vw)], inc="v")

                    def fn_w2(e, c=c, b0=b0, jlo=jlo, WW=WW):
                        return e.tensor_tensor(v32(SCb, 0, P, WW), pb[b0 + 1][0:P, 0:WW],
                                               v16(VB + c * 2 * KW + 2 * (jlo + 1), 0, P, WW), ALU.mult)
                    s.emit("vector", fn_w2, inc="v")

                    def fn_w3(e, WW=WW):
                        return e.tensor_tensor(v32(SCc, 0, P, WW), v32(SCa, 0, P, WW),
                                               v32(SCb, 0, P, WW), ALU.add)
                    s.emit("vector", fn_w3, inc="v")

                    def fn_w4(e, c=c, b0=b0, jlo=jlo, WW=WW):
                        return e.tensor_tensor(v32(SCa, 0, P, WW), pb[b0 + 2][0:P, 0:WW],
                                               v16(VB + c * 2 * KW + 2 * (jlo + 2), 0, P, WW), ALU.mult)
                    v_w4 = s.emit("vector", fn_w4, inc="v")
                    trio_v[b0] = v_w4
                    wfe_trio_guard[b0] = v_w4

                    def fn_w5(e, c=c, jlo=jlo, WW=WW):
                        return e.tensor_tensor(v16(VB + c * 2 * KW + 2 * (jlo - 1), 0, P, WW),
                                               v32(SCc, 0, P, WW),
                                               v32(SCa, 0, P, WW), ALU.add)
                    vw = s.emit("vector", fn_w5, inc="v")

                wn_guard[win_pp] = t_wf
                win_done_at.append(vw)
                return vw

            pending_window.append(emit_window)
            win_pp ^= 1

        while pending_window:
            v_win_done = pending_window.pop(0)(v_win_done, v_subs_prev)

        wb_gu = {}
        wb_wd = {}
        for sb in range(6):
            for kind, srcw, scr in (("g", wg, gscr), ("u", wu, uscr)):
                slot = WVS + (0 if kind == "g" else 8_192)
                w = [("t", t_v_last)]
                if sb >= 1:
                    w.append(("q", wb_gu[(sb - 1, kind)]))
                cw = wdma(
                    lambda slot=slot: v16(slot, 0, P, 4096).rearrange(
                        "p (h m) -> p h m", h=4),
                    lambda srcw=srcw, sb=sb: srcw[4 * sb:4 * sb + 4, :, :].rearrange(
                        "h p m -> p h m"),
                    waits=w)
                wb_gu[(sb, kind)] = sdma(
                    lambda scr=scr, sb=sb: scr[:, 4096 * sb:4096 * sb + 4096],
                    lambda slot=slot: v16(slot, 0, P, 4096),
                    waits=[("w", cw)])
        w_wo = load_attn_weight("wo", WB1, pe_free=max(t_red_last, t_k_last))
        for pi, oc0 in enumerate((2, 4, 6, 0)):
            slot = KB + (pi % 2) * 12_288
            w = [("v", qk_last[0])]
            if pi >= 2:
                w.append(("q", wb_wd[pi - 2]))
            cw = wdma(
                lambda slot=slot: v16(slot, 0, P, 6144).rearrange(
                    "p (o m) -> p o m", o=2),
                lambda oc0=oc0: wdt[oc0:oc0 + 2, :, :].rearrange("o p m -> p o m"),
                waits=w)
            wb_wd[pi] = sdma(
                lambda oc0=oc0: dscr[:, 3072 * oc0:3072 * oc0 + 6144],
                lambda slot=slot: v16(slot, 0, P, 6144),
                waits=[("w", cw)])

        def rhs_vout(ic, t0, W):
            return v16(VB + ic * 2 * KW + 2 * t0, 0, P, W)

        def dst_ao(e, oc, t0, W, ps):
            return e.activation(v32(AO + oc * 4 * 2048 + 4 * t0, 0, P, W), ps[0:P, 0:W],
                                AF.Identity, bias=bias_ap(4, oc))
        t_wo, a_wo_evs = emit_proj(WB1, 4, rhs_vout, dst_ao, G4,
                                   lambda ti: [("v", win_done_at[4])],
                                   mm_extra=[("w", w_wo)],
                                   ev_extra=[("q", wb_gu[(5, "u")])])
        gd["bankev_0"] = a_wo_evs[-2]
        gd["bankev_1"] = a_wo_evs[-1]
        ab_v = []
        for ti2, (t0w, Ww) in enumerate(G4):
            vlast_ab = 0
            for oc in range(CH):
                def fn_ab(e, oc=oc, t0w=t0w, Ww=Ww):
                    return e.tensor_copy(v16(AB + oc * 2 * 2048 + 2 * t0w, 0, P, Ww),
                                         v32(AO + oc * 4 * 2048 + 4 * t0w, 0, P, Ww))
                vlast_ab = s.emit("vector", fn_ab,
                                  waits=[("a", a_wo_evs[ti2 * CH + oc])], inc="v")
            ab_v.append(vlast_ab)

        w_x0a1 = wdma(lambda: v32(X0R + 6 * 4 * 2048, 0, P, 2 * 2048).rearrange(
                          "p (c m) -> p c m", c=2),
                      lambda: x_in[:, 6:8, 1:2049], waits=[("t", t_wo)])
        w_x0a2 = wdma(lambda: v32(X0R + 3 * 4 * 2048, 0, P, 3 * 2048).rearrange(
                          "p (c m) -> p c m", c=3),
                      lambda: x_in[:, 3:6, 1:2049], waits=[("t", t_wo)])
        d_x0r_b = dma(lambda: v32(X0R, 0, P, 3 * 2048).rearrange("p (c m) -> p c m", c=3),
                      lambda: x_in[:, 0:3, 1:2049],
                      waits=[("t", t_wo), ("q", wb_wd[3])])

        lnb_state = {"v": 0, "g": 0, "v67": 0, "t3": None, "tiles": []}
        NPOOL = 0

        lnb_pend = [None]

        def lnb_cb(td):
            t0, W = td["t0"], td["W"]
            if t0 != 1536 and lnb_pend[0] is not None:
                fn, lnb_pend[0] = lnb_pend[0], None
                fn()
            a_evR, a_evB = emit_rb_evac(td, None, as_f32=True)
            rs_, bs_ = td["rslot"], td["bslot"]
            if t0 == 1536:
                lnb_state["t3"] = (td, t0, W, rs_, bs_, a_evR, a_evB)
            else:
                lnb_pend[0] = lambda: lnb_resid(td, t0, W, rs_, bs_, a_evR, a_evB)

        def lnb_resid(td, t0, W, rs_, bs_, a_evR, a_evB,
                      chunks=(6, 7, 0, 1, 2, 3, 4, 5), final=True):
            v_last = 0
            g_last = 0
            for c in chunks:
                pool = c >= CH - NPOOL
                eng, sem, sc = ("gpsimd", "g", T12_P) if pool else ("vector", "v", T12_B)

                def fn_r1(e, c=c, t0=t0, W=W, rs_=rs_, sc=sc):
                    return e.tensor_tensor(v32(sc, 0, P, W),
                                           v32(AO + c * 4 * 2048 + 4 * t0, 0, P, W),
                                           v32(rs_, 0, P, W), ALU.mult)
                wx = (("d", d_x0r_b) if c < 3 else
                      ("w", w_x0a1 if c >= 6 else w_x0a2))
                s.emit(eng, fn_r1,
                       waits=[("a", a_evR), ("a", a_evB), wx], inc=sem)

                def fn_r2(e, W=W, bs_=bs_, sc=sc):
                    return e.tensor_tensor(v32(sc + 2_048, 0, P, W), v32(sc, 0, P, W),
                                           v32(bs_, 0, P, W), ALU.add)
                s.emit(eng, fn_r2, inc=sem)

                def fn_r3(e, c=c, t0=t0, W=W, sc=sc):
                    return e.tensor_tensor(v32(AO + c * 4 * 2048 + 4 * t0, 0, P, W),
                                           v32(sc + 2_048, 0, P, W),
                                           v32(X0R + c * 4 * 2048 + 4 * t0, 0, P, W), ALU.add)
                last = s.emit(eng, fn_r3, inc=sem)
                if pool:
                    g_last = last
                else:
                    v_last = last
                if c == 7 and not pool:
                    lnb_state["v67"] = last
            if not final:
                return
            gd[f"rbt_v{td['par']}"] = v_last
            gd[f"rbt_g{td['par']}"] = g_last
            lnb_state["v"] = v_last
            lnb_state["g"] = g_last
            lnb_state["tiles"].append(v_last)

        emit_ln(AB, 2048, G4, lambda tix: ("v", ab_v[tix]),
                XSQ_B, (2, 3, 4, 5), lnb_cb)
        t3 = lnb_state["t3"]
        lnb_resid(*t3, chunks=(6, 7), final=False)
        fn, lnb_pend[0] = lnb_pend[0], None
        fn()
        lnb_resid(*t3, chunks=(0, 1, 2, 3, 4, 5), final=True)
        v_x1 = lnb_state["v"]
        g_x1 = lnb_state["g"]
        v_x1_tiles = lnb_state["tiles"]

        v_x2 = 0
        g_x2 = 0
        t_down_prev = 0
        o_prev_dma = 0
        nout = [0]
        silu_guard = {}
        h_guard = {}
        gu_mm_guard = {}
        lnd_pend = [None]
        pre_cast = [0]

        def emit_lnc_casts(half_, hbase_, tdp):
            last = 0
            for c in range(CH):
                v_need = (max(v_x1_tiles[1], lnb_state["v67"]) if half_ == 0
                          else v_x1)
                w = [("v", v_need), ("g", g_x1), ("t", tdp),
                     ("t", gd["xsq_t"]), ("q", wb_gu[(5, "u")])]

                def fn(e, c=c, hbase_=hbase_):
                    return e.activation(v16(LNC + c * 2 * 1024, 0, P, 1024),
                                        v32(AO + c * 4 * 2048 + 4 * hbase_, 0, P, 1024),
                                        AF.Copy)
                last = s.emit("scalar", fn, waits=w, inc="a")
            return last
        pending_od = [None]

        def flush_od(eng="sync"):
            if pending_od[0] is not None:
                ods, pending_od[0] = pending_od[0], None
                for fn, w in ods:
                    s.emit(eng, fn, waits=w, inc="o", inc_by=16)
        prev_lnd_evB = 0
        for half in range(2):
            flush_od()
            hbase = half * 1024
            if half == 0:
                a_lnc_cast = emit_lnc_casts(0, 0, t_down_prev)
            else:
                a_lnc_cast = pre_cast[0]
            gridh = [(0, 512), (512, 512)]
            lnc_state = {"v": 0, "tiles": []}

            lnc_pend = [None]

            def lnc_cb(td):
                if lnd_pend[0] is not None:
                    fn, lnd_pend[0] = lnd_pend[0], None
                    fn()
                if lnc_pend[0] is not None:
                    fn, lnc_pend[0] = lnc_pend[0], None
                    fn()
                t0, W = td["t0"], td["W"]
                a_evR, a_evB = emit_rb_evac(td, None, as_f32=False)
                rs_, bs_ = td["rslot"], td["bslot"]
                lnc_pend[0] = lambda: lnc_norm(td, t0, W, rs_, bs_, a_evR, a_evB)

            def lnc_norm(td, t0, W, rs_, bs_, a_evR, a_evB):
                v_last = 0
                for c in range(CH):
                    def fn_n1(e, c=c, t0=t0, W=W, rs_=rs_):
                        return e.tensor_tensor(v16(NSCF, 0, P, W),
                                               v16(LNC + c * 2 * 1024 + 2 * t0, 0, P, W),
                                               v16(rs_, 0, P, W), ALU.mult)
                    s.emit("vector", fn_n1,
                           waits=[("a", a_evR), ("a", a_evB), ("g", s.val("g"))], inc="v")

                    def fn_n2(e, c=c, t0=t0, W=W, bs_=bs_):
                        return e.tensor_tensor(v16(LNC + c * 2 * 1024 + 2 * t0, 0, P, W),
                                               v16(NSCF, 0, P, W),
                                               v16(bs_, 0, P, W), ALU.add)
                    v_last = s.emit("vector", fn_n2, inc="v")
                gd[f"rbt_v{td['par']}"] = v_last
                lnc_state["v"] = v_last
                lnc_state["tiles"].append(v_last)

            emit_ln(LNC, 1024, gridh, lambda tix: ("a", a_lnc_cast), XSQ_F, (0, 1, 4, 5), lnc_cb)
            fn, lnc_pend[0] = lnc_pend[0], None
            fn()
            flush_od()
            v_lnc = lnc_state["v"]
            v_lnc_tiles = lnc_state["tiles"]

            v_h = 0
            w_slab = {}
            for sb in range(6):
                for kind, srcw, scr, bfo in (("g", wg, gscr, GBF), ("u", wu, uscr, UBF)):
                    w = []
                    if sb >= 2 and (sb - 2, kind) in gu_mm_guard:
                        w.append(("t", gu_mm_guard[(sb - 2, kind)]))
                    if sb == 0:
                        w += [("t", s.val("t")), ("a", s.val("a"))]
                    elif sb == 1:
                        w += [("v", s.val("v")), ("a", s.val("a")), ("g", s.val("g")),
                              ("o", o_prev_dma), ("t", s.val("t"))]
                    bslot = bfo + (sb % 2) * 8_192
                    w.append(("q", wb_gu[(sb, kind)]))
                    if kind == "g":
                        cnt = sdma(
                            lambda bslot=bslot: v16(bslot, 0, P, 4096),
                            lambda scr=scr, sb=sb: scr[:, 4096 * sb:4096 * sb + 4096],
                            waits=w)
                        w_slab[(sb, kind)] = ("q", cnt)
                    else:
                        cnt = wdma(
                            lambda bslot=bslot: v16(bslot, 0, P, 4096),
                            lambda scr=scr, sb=sb: scr[:, 4096 * sb:4096 * sb + 4096],
                            waits=w)
                        w_slab[(sb, kind)] = ("w", cnt)
                for hq in range(4):
                    hc = 4 * sb + hq
                    for tt in range(2):
                        t0 = tt * 512
                        gbank, ubank = tt, 2 + tt

                        def fn_gmm(e, sb=sb, hq=hq, t0=t0, gbank=gbank):
                            ins = None
                            for ic in range(CH):
                                ins = e.matmul(pb[gbank][0:P, 0:512],
                                               v16(GBF + (sb % 2) * 8_192 + hq * 2_048
                                                   + 2 * ic * 128, 0, P, 128),
                                               v16(LNC + ic * 2 * 1024 + 2 * t0, 0, P, 512),
                                               start=(ic == 0), stop=(ic == CH - 1))
                            return ins
                        wg_ = [w_slab[(sb, "g")], ("v", v_lnc_tiles[tt])]
                        if (hc - 2, tt) in silu_guard:
                            wg_.append(("a", silu_guard[(hc - 2, tt)]))
                        elif hc <= 1:
                            wg_.append(("a", s.val("a")))
                        t_g = s.emit("tensor", fn_gmm, waits=wg_, inc="t")
                        if hq == 3 and tt == 1:
                            gu_mm_guard[(sb, "g")] = t_g

                        def fn_umm(e, sb=sb, hq=hq, t0=t0, ubank=ubank):
                            ins = None
                            for ic in range(CH):
                                ins = e.matmul(pb[ubank][0:P, 0:512],
                                               v16(UBF + (sb % 2) * 8_192 + hq * 2_048
                                                   + 2 * ic * 128, 0, P, 128),
                                               v16(LNC + ic * 2 * 1024 + 2 * t0, 0, P, 512),
                                               start=(ic == 0), stop=(ic == CH - 1))
                            return ins
                        wu_ = [w_slab[(sb, "u")]]
                        if (hc - 2, tt) in h_guard:
                            wu_.append(("v", h_guard[(hc - 2, tt)]))
                        elif hc <= 1:
                            wu_.append(("v", s.val("v")))
                        t_u = s.emit("tensor", fn_umm, waits=wu_, inc="t")
                        if hq == 3 and tt == 1:
                            gu_mm_guard[(sb, "u")] = t_u

                        sg_slot = SGT + tt * 1_024

                        def fn_silu(e, gbank=gbank, sg_slot=sg_slot):
                            return e.activation(v16(sg_slot, 0, P, 512), pb[gbank][0:P, 0:512],
                                                AF.Silu, scale=DEQ)
                        ws_ = [("t", t_g)]
                        if hc == 0:
                            ws_.append(("g", s.val("g")))
                        a_silu = s.emit("scalar", fn_silu, waits=ws_, inc="a")
                        silu_guard[(hc, tt)] = a_silu

                        def fn_h(e, hc=hc, t0=t0, ubank=ubank, sg_slot=sg_slot):
                            return e.tensor_tensor(v16(HBUF + hc * 2_048 + 2 * t0, 0, P, 512),
                                                   v16(sg_slot, 0, P, 512),
                                                   pb[ubank][0:P, 0:512], ALU.mult)
                        v_h = s.emit("vector", fn_h, waits=[("a", a_silu), ("t", t_u)], inc="v")
                        h_guard[(hc, tt)] = v_h

            wd_slot_guard = [0, 0, 0, 0]
            for tt in range(2):
                t0 = tt * 512
                t_wd_mm = {}
                w_wd = {}
                a_down_last = 0
                oc_order = [2, 3, 4, 5, 6, 7, 0, 1]
                def emit_wd_oc(oci, extra_t):
                    oc = oc_order[oci]
                    slot = WDBF + (oci % 4) * 6_144
                    w = [("v", v_h), ("t", extra_t), ("q", wb_wd[oci // 2])]
                    if oci % 2 == 0:
                        cnt = wdma(lambda slot=slot: v16(slot, 0, P, 3072),
                                   lambda oc=oc: dscr[:, 3072 * oc:3072 * oc + 3072],
                                   waits=w)
                        w_wd[oci] = [("w", cnt)]
                    else:
                        cnt = sdma(lambda slot=slot: v16(slot, 0, P, 3072),
                                   lambda oc=oc: dscr[:, 3072 * oc:3072 * oc + 3072],
                                   waits=w)
                        w_wd[oci] = [("q", cnt)]

                for _oci in range(4):
                    emit_wd_oc(_oci, wd_slot_guard[_oci % 4] or s.val("t"))
                flush_od("scalar")
                for oci, oc in enumerate(oc_order):
                    wdslot = WDBF + (oci % 4) * 6_144

                    def fn_dmm(e, t0=t0, oc=oc, wdslot=wdslot):
                        ins = None
                        for hcc in range(HC):
                            ins = e.matmul(pb[oc][0:P, 0:512],
                                           v16(wdslot + 2 * hcc * 128, 0, P, 128),
                                           v16(HBUF + hcc * 2_048 + 2 * t0, 0, P, 512),
                                           start=(hcc == 0), stop=(hcc == HC - 1))
                        return ins
                    wm = [*w_wd[oci], ("v", v_h)]
                    if oc in (0, 1):
                        wm.append(("a", prev_lnd_evB))
                        wm.append(("a", s.val("a")))
                    t_dmm = s.emit("tensor", fn_dmm, waits=wm, inc="t")
                    t_wd_mm[oci] = t_dmm
                    wd_slot_guard[oci % 4] = t_dmm
                    if oci + 4 <= 7:
                        emit_wd_oc(oci + 4, t_dmm)

                    def fn_dev(e, oc=oc):
                        e.activation(v32(FFO + oc * 4 * 512, 0, P, 512),
                                     pb[oc][0:P, 0:512], AF.Identity, scale=DEQ * DEQ)
                        e.activation(v16(FBT + oc * 2 * 512, 0, P, 512),
                                     pb[oc][0:P, 0:512], AF.Identity, scale=DEQ * DEQ)
                        return e.activation(v16(XSQ_F + oc * 2 * 512, 0, P, 512),
                                            pb[oc][0:P, 0:512], AF.Square, scale=DEQ * DEQ)
                    a_down_last = s.emit("scalar", fn_dev,
                                         waits=[("t", t_dmm), ("o", o_prev_dma)], inc="a")

                if half == 0 and tt == 1:
                    pre_cast[0] = emit_lnc_casts(1, 1024, t_wd_mm[7])

                lnd_state = {"v": 0, "evB": 0}

                def lnd_cb(td):
                    a_evR, a_evB = emit_rb_evac(td, None, as_f32=True)
                    lnd_state["evR"] = a_evR
                    lnd_state["evB"] = a_evB

                lnd_tiles = emit_ln(FBT, 512, [(0, 512)], lambda tix: ("a", a_down_last),
                                    XSQ_F, (0, 1), lnd_cb, skip_square=True)
                td = lnd_tiles[0]
                a_evR, a_evB = lnd_state["evR"], lnd_state["evB"]
                prev_lnd_evB = a_evB

                def lnd_resid(t0=t0, hbase=hbase, a_evR=a_evR, a_evB=a_evB):
                    v_x2 = 0
                    g_x2 = 0
                    pair_done = [0, 0, 0, 0]
                    for c in range(CH):
                        pool = c >= CH - NPOOL
                        eng, sem = ("gpsimd", "g") if pool else ("vector", "v")
                        sc1, sc2 = (NSCF, SGT) if pool else (T12_F, T12_F + 2_048)

                        def fn_r1(e, c=c, sc1=sc1):
                            return e.tensor_tensor(v32(sc1, 0, P, 512),
                                                   v32(FFO + c * 4 * 512, 0, P, 512),
                                                   v32(RBT, 0, P, 512), ALU.mult)
                        s.emit(eng, fn_r1, waits=[("a", a_evR), ("a", a_evB)], inc=sem)

                        def fn_r2(e, sc1=sc1, sc2=sc2):
                            return e.tensor_tensor(v32(sc2, 0, P, 512), v32(sc1, 0, P, 512),
                                                   v32(RBT + 2_080, 0, P, 512), ALU.add)
                        s.emit(eng, fn_r2, inc=sem)

                        def fn_r3(e, c=c, t0=t0, hbase=hbase, sc2=sc2):
                            return e.tensor_tensor(v32(FFO + c * 4 * 512, 0, P, 512),
                                                   v32(sc2, 0, P, 512),
                                                   v32(AO + c * 4 * 2048 + 4 * (hbase + t0), 0, P, 512),
                                                   ALU.add)
                        if pool:
                            g_x2 = s.emit(eng, fn_r3, inc=sem)
                        else:
                            v_x2 = s.emit(eng, fn_r3, inc=sem)
                        if c % 2 == 1 and not pool:
                            pair_done[c // 2] = v_x2
                    gd["rbt_v0"] = v_x2
                    gd["rbt_g0"] = g_x2

                    ods = []
                    for k in range(4):
                        def fn_od(e, t0=t0, hbase=hbase, k=k):
                            return e.dma_start(
                                out=out[:, 2 * k:2 * k + 2, hbase + t0: hbase + t0 + 512],
                                in_=v32(FFO + k * 2 * 4 * 512, 0, P, 2 * 512).rearrange(
                                    "p (c m) -> p c m", c=2))
                        w = [("v", pair_done[k])]
                        if k == 3:
                            w.append(("g", g_x2))
                        ods.append((fn_od, w))
                    pending_od[0] = ods

                nout[0] += 4
                o_prev_dma = 16 * nout[0]
                if tt == 0:
                    lnd_resid()
                else:
                    lnd_pend[0] = lnd_resid
                t_down_prev = t_wd_mm[7]

        if lnd_pend[0] is not None:
            fn, lnd_pend[0] = lnd_pend[0], None
            fn()
        flush_od()
        if _kdump is not None:
            s.trunc = None
            doff = int(_kdump)
            allw = [(k, s.val(k)) for k in ("t", "a", "v", "g")] + \
                   [("d", s.val("d")), ("w", s.val("w")), ("o", s.val("o"))]

            def fn_dump(e, doff=doff):
                return e.dma_start(out=out[:],
                                   in_=v32(doff, 0, P, CH * 2048).rearrange(
                                       "p (c m) -> p c m", c=CH))
            s.emit("sync", fn_dump, waits=allw, inc="o", inc_by=16)
            nout[0] += 1
        s.emit("sync", lambda e: e.nop(), waits=[("o", 16 * nout[0])])

        @block.sync
        def _(sync):
            s.run("sync", sync, sems)

        @block.tensor
        def _(tensor):
            s.run("tensor", tensor, sems)

        @block.scalar
        def _(scalar):
            s.run("scalar", scalar, sems)

        @block.vector
        def _(vector):
            s.run("vector", vector, sems)

        @block.gpsimd
        def _(gpsimd):
            s.run("gpsimd", gpsimd, sems)

    return nc


_BF = ml_dtypes.bfloat16


def _tile_weight(w):
    wt = np.ascontiguousarray(w.T)
    return np.ascontiguousarray(wt.reshape(CH, P, w.shape[0]).transpose(1, 0, 2))


def _tile_gu(wint):
    wt = np.ascontiguousarray(wint.T)
    a = wt.reshape(CH, P, HC, P).transpose(2, 1, 0, 3)
    return np.ascontiguousarray(a.reshape(HC, P, CH * P))


def _tile_wd(wint):
    wt = np.ascontiguousarray(wint.T)
    a = wt.reshape(HC, P, CH, P).transpose(2, 1, 0, 3)
    return np.ascontiguousarray(a.reshape(CH, P, HC * P))


def kernel(L0_attn_x, L0_attn_Wp, L0_attn_bp, L0_attn_Wq, L0_attn_bq, L0_attn_Wk,
           L0_attn_bk, L0_attn_Wv, L0_attn_bv, L0_attn_Wo, L0_attn_bo,
           L0_ffn_w_gate, L0_ffn_w_up, L0_ffn_w_down):
    x = np.asarray(L0_attn_x, dtype=np.float32)
    S = x.shape[1]
    xt = np.ascontiguousarray(x[0].T)

    wp = _tile_weight(np.asarray(L0_attn_Wp, np.float32))
    wq = _tile_weight(np.asarray(L0_attn_Wq, np.float32))
    wk = _tile_weight(np.asarray(L0_attn_Wk, np.float32))
    wv = _tile_weight(np.asarray(L0_attn_Wv, np.float32))
    wo = _tile_weight(np.asarray(L0_attn_Wo, np.float32))
    wg_t = _tile_gu(np.asarray(L0_ffn_w_gate, np.int32))
    wu_t = _tile_gu(np.asarray(L0_ffn_w_up, np.int32))
    wd_t = _tile_wd(np.asarray(L0_ffn_w_down, np.int32))

    bias_all = np.zeros((P, CH * 5), np.float32)
    for wi_, b in enumerate((L0_attn_bp, L0_attn_bq, L0_attn_bk, L0_attn_bv, L0_attn_bo)):
        b = np.asarray(b, np.float32).reshape(CH, P)
        for oc in range(CH):
            bias_all[:, oc * 5 + wi_] = b[oc]

    e16 = np.zeros((P, CH * H), np.float32)
    for c in range(CH):
        for pp_ in range(P):
            e16[pp_, c * H + 2 * c + (pp_ // 64)] = 1.0
    eb = np.zeros((H, CH * P), np.float32)
    for c in range(CH):
        for pp_ in range(P):
            eb[2 * c + (pp_ // 64), c * P + pp_] = 1.0

    nc = build_program()
    in_maps = []
    for core in range(NCORE):
        start = core * T
        idx = (np.arange(start - 1, start + T + 1) % S)
        xc = np.ascontiguousarray(xt[:, idx])
        x_tiled = np.ascontiguousarray(xc.reshape(CH, P, THALO).transpose(1, 0, 2))
        in_maps.append(dict(
            x_in=x_tiled, wp=wp, wk=wk, wv=wv, wq=wq, wo=wo,
            biases=bias_all, e16=e16.astype(_BF), ebc=eb.astype(_BF),
            wg=wg_t, wu=wu_t, wdt=wd_t,
        ))
    res = run_bass_kernel_spmd(nc, in_maps, core_ids=list(range(NCORE)))
    shards = []
    for core in range(NCORE):
        o = np.asarray(res.results[core]["out"])
        shards.append(o.transpose(1, 0, 2).reshape(D, T))
    y = np.concatenate(shards, axis=1)
    return np.ascontiguousarray(y.T)[None, :, :]
